# version 22
# baseline (speedup 1.0000x reference)
import numpy as np
import concourse.bass as bass
import concourse.mybir as mybir
from concourse.bass_utils import run_bass_kernel_spmd
from concourse.bass_core_rust import IndirectOffsetOnAxis

F32 = mybir.dt.float32
BF16 = mybir.dt.bfloat16
I32 = mybir.dt.int32
AF = mybir.ActivationFunctionType
ALU = mybir.AluOpType
AX = mybir.AxisListType.X

S = 4096
D = 1024
H = 4096
E = 8
NDC = D // 128
NTT = S // 128
CAP = 1280
NJ = CAP // 128
AUGW = D + 2 * E
NSLOT = 1024 * 11
NHQ = H // 128 // 4
SCALE = float(np.sqrt(0.5))
EPS = 1e-5
BIG = 1e30

ENGINES = ("sync", "gpsimd", "vector", "scalar", "tensor")


class Sched:

    def __init__(self, nc, semnames):
        self.nc = nc
        self.q = {e: [] for e in ENGINES}
        self.cnt = {s: 0 for s in semnames}
        self.sem = {}

    def op(self, eng, build=None, wait=(), inc=None):
        w = tuple((s, v) for (s, v) in wait if v and v > 0)
        tok = None
        if inc is not None:
            self.cnt[inc[0]] += inc[1]
            tok = self.cnt[inc[0]]

        def closure(e, _w=w, _b=build, _i=inc):
            for s, v in _w:
                e.wait_ge(self.sem[s], v)
            if _b is not None:
                ins = _b(e)
                if _i is not None:
                    ins.then_inc(self.sem[_i[0]], _i[1])
            elif _i is not None:
                e.nop().then_inc(self.sem[_i[0]], _i[1])

        self.q[eng].append(closure)
        return tok

    def flush_block(self):
        with self.nc.Block() as block:
            for e in ENGINES:
                if self.q[e]:
                    ops = self.q[e]
                    self.q[e] = []
                    getattr(block, e)(lambda eng, ops=ops: [f(eng) for f in ops])


def emit_identity(sc, identity):
    sq = identity.shape[0]
    tm = sc.op("gpsimd", lambda e: e.memset(identity, 0.0), inc=("dW", 1))
    sc.op("gpsimd", lambda e: e.affine_select(
        out=identity, in_=identity, compare_op=ALU.not_equal,
        fill=1.0, base=0, pattern=[[-1, sq]], channel_multiplier=1),
        wait=[("dW", tm)])


def build_nc(milestone=3):
    nc = bass.Bass("TRN2", target_bir_lowering=False)

    x_d = nc.dram_tensor("x", [S, D], F32, kind="ExternalInput")
    w1_d = nc.dram_tensor("w1t", [E, D, H], F32, kind="ExternalInput")
    w3_d = nc.dram_tensor("w3t", [E, D, H], F32, kind="ExternalInput")
    w2_d = nc.dram_tensor("w2t", [E, H, D], F32, kind="ExternalInput")
    gw_d = nc.dram_tensor("gwt", [D, E], F32, kind="ExternalInput")
    delta_d = nc.dram_tensor("delta_t", [128, 2 * NDC], F32, kind="ExternalInput")
    alpha_d = nc.dram_tensor("alpha_t", [128, 2 * NDC], F32, kind="ExternalInput")
    beta_d = nc.dram_tensor("beta_t", [128, 2 * NDC], F32, kind="ExternalInput")
    gamma_d = nc.dram_tensor("gamma_t", [128, 2 * NDC], F32, kind="ExternalInput")
    omega_d = nc.dram_tensor("omega_t", [128, NDC], F32, kind="ExternalInput")
    out_d = nc.dram_tensor("out", [S, D], F32, kind="ExternalOutput")
    aug_d = nc.dram_tensor("xn_aug", [S, AUGW], BF16, kind="Internal")
    slots_d = nc.dram_tensor(
        "slots", [NSLOT], I32,
        kind="ExternalOutput" if milestone in (1, 2) else "Internal")
    yall_d = nc.dram_tensor("yall", [E * CAP, D], F32,
                            kind="ExternalOutput" if milestone == 2 else "Internal")
    if milestone == 2:
        dbg_xc = nc.dram_tensor("dbg_xc", [128, NJ, AUGW], BF16, kind="ExternalOutput")
        dbg_xnc = nc.dram_tensor("dbg_xnc", [128, NDC, CAP], BF16,
                                 kind="ExternalOutput")
        dbg_h = nc.dram_tensor("dbg_h", [128, NHQ, CAP], BF16, kind="ExternalOutput")
    if milestone == 1:
        dbg_g = nc.dram_tensor("dbg_g", [128, 4 * NTT], I32, kind="ExternalOutput")
        dbg_cb = nc.dram_tensor("dbg_cb", [128, NTT * E], F32, kind="ExternalOutput")

    x_tiles = x_d.ap().rearrange("(tt p) d -> tt p d", p=128)
    out_rows = out_d.ap().rearrange("(tt p) d -> tt p d", p=128)
    out_stage = out_d.ap().rearrange("(tg tts p) d -> tg p tts d", tts=4, p=128)
    x_stage = x_d.ap().rearrange("(tg tts p) d -> tg p tts d", tts=4, p=128)
    aug_rows = aug_d.ap().rearrange("(tt p) c -> tt p c", p=128)
    gw_v = gw_d.ap().rearrange("(dc p) e -> p dc e", p=128)
    w1_v = w1_d.ap().rearrange("e (dc p) h -> e p dc h", p=128)
    w3_v = w3_d.ap().rearrange("e (dc p) h -> e p dc h", p=128)
    w2_v = w2_d.ap().rearrange("e (hc p) d -> e p hc d", p=128)
    yall_rows = yall_d.ap().rearrange("(r p) d -> r p d", p=128)
    slots_v = slots_d.ap().rearrange("(sl e) -> sl e", e=E)
    slots_sbv = slots_d.ap().rearrange("(jj p e) -> p jj e", p=128, e=E)

    sems = ["dX", "dW", "dO", "tS", "vS", "sS",
            "x0", "x1", "st0", "st1", "xs", "dA", "dS", "dG0", "dG1", "dY",
            "w10", "w11", "w30", "w31", "w20", "w21",
            "fb0", "fb1", "fb2", "fb3", "fg0", "fg1", "fg2", "fg3",
            "fs0", "fs1", "fs2", "fs3"]
    sc = Sched(nc, sems)

    import contextlib
    ctx = contextlib.ExitStack()
    with ctx:
        for s in sems:
            sc.sem[s] = ctx.enter_context(nc.semaphore(s))

        s_col = ctx.enter_context(nc.sbuf_tensor("s_col", [128, NTT], F32))
        ss_col = ctx.enter_context(nc.sbuf_tensor("ss_col", [128, NTT], F32))
        mu_col = ctx.enter_context(nc.sbuf_tensor("mu_col", [128, NTT], F32))
        rstd_col = ctx.enter_context(nc.sbuf_tensor("rstd_col", [128, NTT], F32))
        var_col = ctx.enter_context(nc.sbuf_tensor("var_col", [128, NTT], F32))
        scores = ctx.enter_context(nc.sbuf_tensor("scores", [128, NTT, E], F32))
        comb = ctx.enter_context(nc.sbuf_tensor("comb", [128, NTT, E], F32))
        gmask1 = ctx.enter_context(nc.sbuf_tensor("gmask1", [128, NTT, E], F32))
        gmask2 = ctx.enter_context(nc.sbuf_tensor("gmask2", [128, NTT, E], F32))
        gsc = ctx.enter_context(nc.sbuf_tensor("gsc", [128, NTT, E], F32))
        gms = ctx.enter_context(nc.sbuf_tensor("gms", [128, NTT, E], F32))
        gm1 = ctx.enter_context(nc.sbuf_tensor("gm1", [128, NTT, 1], F32))
        gm2 = ctx.enter_context(nc.sbuf_tensor("gm2", [128, NTT, 1], F32))
        gw1g = ctx.enter_context(nc.sbuf_tensor("gw1g", [128, NTT, 1], F32))
        gw2g = ctx.enter_context(nc.sbuf_tensor("gw2g", [128, NTT, 1], F32))
        comb_hl = ctx.enter_context(nc.sbuf_tensor("comb_hl", [128, NTT, E, 2], BF16))
        gwf = ctx.enter_context(nc.sbuf_tensor("gwf", [128, NDC, E], F32))
        gwhi = ctx.enter_context(nc.sbuf_tensor("gwhi", [128, NDC, E], BF16))
        gwlo = ctx.enter_context(nc.sbuf_tensor("gwlo", [128, NDC, E], BF16))
        prm_delta = ctx.enter_context(nc.sbuf_tensor("prm_delta", [128, 2 * NDC], F32))
        prm_alpha = ctx.enter_context(nc.sbuf_tensor("prm_alpha", [128, 2 * NDC], F32))
        prm_beta = ctx.enter_context(nc.sbuf_tensor("prm_beta", [128, 2 * NDC], F32))
        prm_gamma = ctx.enter_context(nc.sbuf_tensor("prm_gamma", [128, 2 * NDC], F32))
        prm_omega = ctx.enter_context(nc.sbuf_tensor("prm_omega", [128, NDC], F32))
        q_t = ctx.enter_context(nc.sbuf_tensor("q_t", [128, 2 * NDC], F32))
        c_t = ctx.enter_context(nc.sbuf_tensor("c_t", [128, 2 * NDC], F32))
        tmp_p = ctx.enter_context(nc.sbuf_tensor("tmp_p", [128, 2 * NDC], F32))
        id_bf = ctx.enter_context(nc.sbuf_tensor("id_bf", [128, 128], BF16))
        id_f32 = ctx.enter_context(nc.sbuf_tensor("id_f32", [128, 128], F32))
        epsb = ctx.enter_context(nc.sbuf_tensor("epsb", [128, 1], F32))
        iota_e = ctx.enter_context(nc.sbuf_tensor("iota_e", [128, E], F32))
        iota_tok = ctx.enter_context(nc.sbuf_tensor("iota_tok", [128, NTT], I32))
        zslot = ctx.enter_context(nc.sbuf_tensor("zslot", [128, NSLOT // 128], I32))
        pos_tok = ctx.enter_context(nc.sbuf_tensor("pos_tok", [128, NTT, E], F32))
        g1i = ctx.enter_context(nc.sbuf_tensor("g1i", [128, NTT], I32))
        g2i = ctx.enter_context(nc.sbuf_tensor("g2i", [128, NTT], I32))
        h1i = ctx.enter_context(nc.sbuf_tensor("h1i", [128, NTT], I32))
        h2i = ctx.enter_context(nc.sbuf_tensor("h2i", [128, NTT], I32))
        ctx_xnT = contextlib.ExitStack()
        xnT = ctx_xnT.enter_context(nc.sbuf_tensor("xnT", [128, NDC, S], BF16))

        t_prm = None
        for (src, dst) in ((delta_d.ap(), prm_delta), (alpha_d.ap(), prm_alpha),
                           (beta_d.ap(), prm_beta), (gamma_d.ap(), prm_gamma),
                           (omega_d.ap(), prm_omega), (gw_v, gwf)):
            t_prm = sc.op("sync", lambda e, s=src, d=dst: e.dma_start(out=d[:], in_=s),
                          inc=("dX", 16))
        emit_identity(sc, id_bf[:])
        emit_identity(sc, id_f32[:])
        sc.op("gpsimd", lambda e: e.memset(epsb[:], EPS), inc=("dW", 1))
        sc.op("gpsimd", lambda e: e.iota(
            iota_e[:], [[1, E]], channel_multiplier=0,
            allow_small_or_imprecise_dtypes=True), inc=("dW", 1))
        sc.op("gpsimd", lambda e: e.iota(iota_tok[:], [[128, NTT]],
                                         channel_multiplier=1), inc=("dW", 1))
        t_ident = sc.op("gpsimd", lambda e: e.memset(zslot[:], 0), inc=("dW", 1))
        t_zs = sc.op("gpsimd", lambda e: e.dma_start(
            out=slots_d.ap().rearrange("(p c) -> p c", p=128), in_=zslot[:]),
            wait=[("dW", t_ident)], inc=("dS", 16))

        sc.op("scalar", lambda e: e.activation(out=q_t[:], in_=prm_delta[:],
                                               func=AF.Sigmoid), wait=[("dX", t_prm)])
        t_sig = sc.op("scalar", lambda e: e.activation(
            out=tmp_p[:], in_=prm_alpha[:], func=AF.Sigmoid), inc=("sS", 1))
        v1 = sc.op("vector", lambda e: e.tensor_mul(tmp_p[:], q_t[:], tmp_p[:]),
                   wait=[("sS", t_sig)], inc=("vS", 1))
        v2 = sc.op("vector", lambda e: e.tensor_mul(c_t[:], q_t[:], prm_beta[:]),
                   inc=("vS", 1))
        sc.op("vector", lambda e: e.tensor_scalar(
            out=q_t[:], in0=tmp_p[:], scalar1=-1.0, scalar2=1.0,
            op0=ALU.mult, op1=ALU.add), wait=[("vS", v2)])
        v4 = sc.op("vector", lambda e: e.tensor_mul(c_t[:], c_t[:], prm_gamma[:]),
                   wait=[("vS", v2)], inc=("vS", 1))
        sc.op("vector", lambda e: e.tensor_scalar_mul(c_t[:], c_t[:], SCALE),
              wait=[("vS", v4)])
        v6 = sc.op("vector", lambda e: e.tensor_copy(out=gwhi[:], in_=gwf[:]),
                   inc=("vS", 1))
        sc.op("vector", lambda e: e.tensor_sub(gwlo[:], gwf[:], gwhi[:]),
              wait=[("vS", v6)])
        sc.flush_block()

        with (
            nc.sbuf_tensor("x_tile", [128, 2, D], F32) as x_tile,
            nc.sbuf_tensor("sqdump", [128, 2, D], F32) as sqdump,
        ):
            with (
                nc.sbuf_tensor("xnf", [128, D], F32) as xnf,
                nc.sbuf_tensor("aug_sb", [128, 2, D], BF16) as aug_sb,
                nc.sbuf_tensor("xn_lo", [128, 2, D], BF16) as xn_lo,
                nc.sbuf_tensor("xnlo_t", [128, 2, D], BF16) as xnlo_t,
                nc.psum_tensor("pst_hi", [128, D], BF16) as pst_hi,
                nc.psum_tensor("pst_lo", [128, D], BF16) as pst_lo,
                nc.psum_tensor("psc0", [128, E], F32) as psc0,
                nc.psum_tensor("psc1", [128, E], F32) as psc1,
            ):
                psc = (psc0, psc1)
                tok_tT = {}
                tok_cp = {}
                tok_scp = {}
                aug_tok = {}
                t_rs_prev = 0
                for tt in range(NTT):
                    b = tt % 2
                    xsn = "x%d" % b
                    t_dma = sc.op("sync", lambda e, tt=tt, b=b: e.dma_start(
                        out=x_tile[:, b, :], in_=x_tiles[tt]),
                        wait=[("vS", tok_cp.get(tt - 2, 0)),
                              ("sS", t_rs_prev)], inc=(xsn, 16))
                    ml = (tt, tt + 1)
                    t_red = sc.op("vector", lambda e, tt=tt, b=b: e.reduce_sum(
                        out=s_col[:, tt:tt + 1], in_=x_tile[:, b, :], axis=AX),
                        wait=[(xsn, t_dma)], inc=("vS", 1))
                    t_sq = sc.op("scalar", lambda e, tt=tt, b=b: e.activation(
                        out=sqdump[:, b, :], in_=x_tile[:, b, :], func=AF.Square,
                        accum_out=ss_col[:, tt:tt + 1]),
                        wait=[(xsn, t_dma), ("sS", t_rs_prev)], inc=("sS", 1))
                    v_mu = sc.op("vector", lambda e, tt=tt: e.tensor_scalar_mul(
                        mu_col[:, tt:tt + 1], s_col[:, tt:tt + 1], 1.0 / D),
                        wait=[("vS", t_red)], inc=("vS", 1))
                    v_ms = sc.op("vector", lambda e, tt=tt: e.tensor_scalar_mul(
                        var_col[:, tt:tt + 1], ss_col[:, tt:tt + 1], 1.0 / D),
                        wait=[("sS", t_sq)], inc=("vS", 1))
                    v_msq = sc.op("vector", lambda e, tt=tt: e.tensor_mul(
                        s_col[:, tt:tt + 1], mu_col[:, tt:tt + 1],
                        mu_col[:, tt:tt + 1]),
                        wait=[("vS", v_mu)], inc=("vS", 1))
                    v_var = sc.op("vector", lambda e, tt=tt: e.tensor_sub(
                        var_col[:, tt:tt + 1], var_col[:, tt:tt + 1],
                        s_col[:, tt:tt + 1]),
                        wait=[("vS", v_msq)], inc=("vS", 1))
                    s_ln = sc.op("scalar", lambda e, tt=tt: e.activation(
                        out=var_col[:, tt:tt + 1], in_=var_col[:, tt:tt + 1],
                        func=AF.Ln, bias=epsb[:, 0:1]),
                        wait=[("vS", v_var), ("dW", t_ident)], inc=("sS", 1))
                    t_rs = sc.op("scalar", lambda e, tt=tt: e.activation(
                        out=rstd_col[:, tt:tt + 1], in_=var_col[:, tt:tt + 1],
                        func=AF.Exp, scale=-0.5),
                        wait=[("sS", s_ln)], inc=("sS", 1))
                    t_rs_prev = t_rs
                    wv = [("sS", t_rs)]
                    if tt >= 2:
                        wv.append(("tS", tok_tT[tt - 2]))
                        wv.append(aug_tok[tt - 2])
                    t_xnf = sc.op("vector", lambda e, tt=tt, b=b: e.tensor_scalar(
                        out=xnf[:], in0=x_tile[:, b, :],
                        scalar1=mu_col[:, tt:tt + 1], scalar2=rstd_col[:, tt:tt + 1],
                        op0=ALU.subtract, op1=ALU.mult), wait=wv, inc=("vS", 1))
                    t_hic = sc.op("vector", lambda e, b=b: e.tensor_copy(
                        out=aug_sb[:, b, :], in_=xnf[:]),
                        wait=[("vS", t_xnf)], inc=("vS", 1))
                    t_hilo = sc.op("vector", lambda e, b=b: e.tensor_sub(
                        xn_lo[:, b, :], xnf[:], aug_sb[:, b, :]),
                        wait=[("vS", t_hic)], inc=("vS", 1))
                    aug_tok[tt] = ("st%d" % b, sc.op(
                        "sync", lambda e, tt=tt, b=b: e.dma_start(
                            out=aug_rows[tt][:, 0:D], in_=aug_sb[:, b, :]),
                        wait=[("vS", t_hilo)], inc=("st%d" % b, 16)))
                    wt = [("vS", t_hilo), ("dW", t_ident)]
                    if tt >= 1:
                        wt.append(("vS", tok_cp[tt - 1]))
                    for dc in range(NDC):
                        tok_tT[tt] = sc.op("tensor", lambda e, dc=dc, b=b: e.transpose(
                            pst_hi[:, dc * 128:(dc + 1) * 128],
                            aug_sb[:, b, dc * 128:(dc + 1) * 128], id_bf[:]),
                            wait=(wt if dc == 0 else []), inc=("tS", 1))
                    for dc in range(NDC):
                        tok_tT[tt] = sc.op("tensor", lambda e, dc=dc, b=b: e.transpose(
                            pst_lo[:, dc * 128:(dc + 1) * 128],
                            xn_lo[:, b, dc * 128:(dc + 1) * 128], id_bf[:]),
                            inc=("tS", 1))
                    sc.op("vector", lambda e, tt=tt: e.tensor_copy(
                        out=xnT[:, :, tt * 128:(tt + 1) * 128],
                        in_=pst_hi[:].rearrange("p (dc t) -> p dc t", t=128)),
                        wait=[("tS", tok_tT[tt])])
                    tok_cp[tt] = sc.op("vector", lambda e, b=b: e.tensor_copy(
                        out=xnlo_t[:, b, :], in_=pst_lo[:]), inc=("vS", 1))
                    wg = [("vS", tok_cp[tt])]
                    if tt >= 2:
                        wg.append(("vS", tok_scp[tt - 2]))
                    t_gmm = None
                    n_gmm = 3 * NDC
                    gi = 0
                    for dc in range(NDC):
                        hi_ap = xnT[:, dc, tt * 128:(tt + 1) * 128]
                        lo_ap = xnlo_t[:, b, dc * 128:(dc + 1) * 128]
                        for (lhs, rhs) in ((hi_ap, gwhi), (hi_ap, gwlo), (lo_ap, gwhi)):
                            t_gmm = sc.op("tensor", lambda e, l=lhs, r=rhs, dc=dc,
                                          gi=gi, pt=psc[tt % 2]: e.matmul(
                                pt[:], l, r[:, dc, :], start=(gi == 0),
                                stop=(gi == n_gmm - 1)),
                                wait=(wg if gi == 0 else []), inc=("tS", 1))
                            gi += 1
                    tok_scp[tt] = sc.op("vector", lambda e, tt=tt, pt=psc[tt % 2]:
                                        e.tensor_copy(out=scores[:, tt, :], in_=pt[:]),
                                        wait=[("tS", t_gmm)], inc=("vS", 1))
                sc.flush_block()

        g1t = sc.op("vector", lambda e: e.reduce_max(out=gm1[:], in_=scores[:], axis=AX),
                    inc=("vS", 1))
        g2t = sc.op("vector", lambda e: e.tensor_tensor(
            out=gmask1[:], in0=scores[:],
            in1=gm1[:].broadcast_to((128, NTT, E)), op=ALU.is_equal),
            wait=[("vS", g1t)], inc=("vS", 1))
        g3t = sc.op("vector", lambda e: e.scalar_tensor_tensor(
            out=gsc[:], in0=gmask1[:], scalar=-BIG, in1=scores[:],
            op0=ALU.mult, op1=ALU.add), wait=[("vS", g2t)], inc=("vS", 1))
        g4t = sc.op("vector", lambda e: e.reduce_max(out=gm2[:], in_=gsc[:], axis=AX),
                    wait=[("vS", g3t)], inc=("vS", 1))
        g5t = sc.op("vector", lambda e: e.tensor_tensor(
            out=gmask2[:], in0=gsc[:],
            in1=gm2[:].broadcast_to((128, NTT, E)), op=ALU.is_equal),
            wait=[("vS", g4t)], inc=("vS", 1))
        g5u = sc.op("vector", lambda e: e.tensor_add(gms[:], gmask1[:], gmask2[:]),
                    wait=[("vS", g5t)], inc=("vS", 1))
        t_d12 = sc.op("vector", lambda e: e.tensor_tensor(
            out=gw1g[:], in0=gm1[:], in1=gm2[:], op=ALU.subtract),
            wait=[("vS", g4t)], inc=("vS", 1))
        t_sg = sc.op("scalar", lambda e: e.activation(
            out=gw1g[:], in_=gw1g[:], func=AF.Sigmoid),
            wait=[("vS", t_d12)], inc=("sS", 1))
        g6t = sc.op("vector", lambda e: e.tensor_scalar(
            out=gw2g[:], in0=gw1g[:], scalar1=-1.0, scalar2=1.0,
            op0=ALU.mult, op1=ALU.add), wait=[("sS", t_sg)], inc=("vS", 1))
        g7t = sc.op("vector", lambda e: e.tensor_tensor(
            out=gsc[:], in0=gmask1[:],
            in1=gw1g[:].broadcast_to((128, NTT, E)), op=ALU.mult),
            wait=[("sS", t_sg)], inc=("vS", 1))
        g8t = sc.op("vector", lambda e: e.tensor_tensor(
            out=comb[:], in0=gmask2[:],
            in1=gw2g[:].broadcast_to((128, NTT, E)), op=ALU.mult),
            wait=[("vS", max(g6t, g5t))], inc=("vS", 1))
        g9t = sc.op("vector", lambda e: e.tensor_add(comb[:], comb[:], gsc[:]),
                    wait=[("vS", max(g8t, g7t))], inc=("vS", 1))
        gat = sc.op("vector", lambda e: e.tensor_copy(
            out=comb_hl[:, :, :, 0], in_=comb[:]),
            wait=[("vS", g9t)], inc=("vS", 1))
        gbt = sc.op("vector", lambda e: e.tensor_sub(
            comb_hl[:, :, :, 1], comb[:], comb_hl[:, :, :, 0]),
            wait=[("vS", gat)], inc=("vS", 1))
        t_combdma = sc.op("sync", lambda e: e.dma_start(
            out=aug_d.ap().rearrange("(tt p) c -> p tt c", p=128)[:, :, D:AUGW],
            in_=comb_hl[:].rearrange("p tt e hl -> p tt (e hl)")),
            wait=[("vS", gbt)], inc=("dA", 16))
        sc.flush_block()

        a2_dma_done = [("st0", sc.cnt["st0"]), ("st1", sc.cnt["st1"]),
                       ("dA", sc.cnt["dA"])]
        with (
            nc.sbuf_tensor("combT", [E, S], F32) as combT,
            nc.sbuf_tensor("maskT", [E, S], F32) as maskT,
            nc.sbuf_tensor("posT", [E, S], F32) as posT,
            nc.sbuf_tensor("ones8", [E, 1], F32) as ones8,
            nc.sbuf_tensor("gf", [128, 2, NTT], F32) as gf,
            nc.sbuf_tensor("rsc", [128, NTT, E], F32) as rsc,
            nc.psum_tensor("prt0", [E, 128], F32) as prt0,
            nc.psum_tensor("prt1", [E, 128], F32) as prt1,
            nc.psum_tensor("prb0", [128, E], F32) as prb0,
            nc.psum_tensor("prb1", [128, E], F32) as prb1,
        ):
            prt = (prt0, prt1)
            prb = (prb0, prb1)
            t_on8 = sc.op("gpsimd", lambda e: e.memset(ones8[:], 1.0),
                          wait=a2_dma_done, inc=("dW", 1))
            tok_ct = {}
            tok_cc = {}
            for tt in range(NTT):
                tok_ct[tt] = sc.op("tensor", lambda e, tt=tt, pt=prt[tt % 2]:
                                   e.matmul(
                    pt[:], gms[:, tt, :], id_f32[:], start=True, stop=True),
                    wait=[("vS", tok_cc.get(tt - 2, 0) if tt >= 2 else g5u)],
                    inc=("tS", 1))
                tok_cc[tt] = sc.op("vector", lambda e, tt=tt, pt=prt[tt % 2]:
                                   e.tensor_copy(
                    out=combT[:, tt * 128:(tt + 1) * 128], in_=pt[:]),
                    wait=[("tS", tok_ct[tt])] + (a2_dma_done if tt == 0 else []),
                    inc=("vS", 1))
            r1 = sc.op("vector", lambda e: e.tensor_single_scalar(
                out=maskT[:], in_=combT[:], scalar=0.5, op=ALU.is_gt),
                wait=[("vS", tok_cc[NTT - 1])], inc=("vS", 1))
            r2 = sc.op("vector", lambda e: e.tensor_tensor_scan(
                out=posT[:], data0=ones8[:, 0:1].broadcast_to((E, S)),
                data1=maskT[:], initial=0.0, op0=ALU.mult, op1=ALU.add),
                wait=[("vS", r1), ("dW", t_on8)], inc=("vS", 1))
            tok_bt = {}
            tok_bc = {}
            for tt in range(NTT):
                tok_bt[tt] = sc.op("tensor", lambda e, tt=tt, pb=prb[tt % 2]:
                                   e.matmul(
                    pb[:], posT[:, tt * 128:(tt + 1) * 128],
                    id_f32[0:E, 0:E], start=True, stop=True),
                    wait=[("vS", r2 if tt < 2 else tok_bc[tt - 2])], inc=("tS", 1))
                tok_bc[tt] = sc.op("vector", lambda e, tt=tt, pb=prb[tt % 2]:
                                   e.tensor_copy(
                    out=pos_tok[:, tt, :], in_=pb[:]),
                    wait=[("tS", tok_bt[tt])], inc=("vS", 1))
            q1 = sc.op("vector", lambda e: e.tensor_scalar_add(
                pos_tok[:], pos_tok[:], -1.0),
                wait=[("vS", tok_bc[NTT - 1])], inc=("vS", 1))
            q2 = sc.op("vector", lambda e: e.scalar_tensor_tensor(
                out=rsc[:], in0=pos_tok[:], scalar=float(E),
                in1=iota_e[:].unsqueeze(1).broadcast_to((128, NTT, E)),
                op0=ALU.mult, op1=ALU.add), wait=[("vS", q1)], inc=("vS", 1))
            q3 = sc.op("vector", lambda e: e.tensor_mul(gsc[:], rsc[:], gmask1[:]),
                       wait=[("vS", q2)], inc=("vS", 1))
            q4 = sc.op("vector", lambda e: e.reduce_sum(
                out=gf[:, 0, :].unsqueeze(2), in_=gsc[:], axis=AX),
                wait=[("vS", q3)], inc=("vS", 1))
            q5 = sc.op("vector", lambda e: e.tensor_scalar_min(
                gf[:, 0, :], gf[:, 0, :], float(CAP * E)),
                wait=[("vS", q4)], inc=("vS", 1))
            q6 = sc.op("vector", lambda e: e.tensor_copy(out=h1i[:], in_=gf[:, 0, :]),
                       wait=[("vS", q5)], inc=("vS", 1))
            q7 = sc.op("vector", lambda e: e.tensor_mul(gsc[:], rsc[:], gmask2[:]),
                       wait=[("vS", q6)], inc=("vS", 1))
            q8 = sc.op("vector", lambda e: e.reduce_sum(
                out=gf[:, 1, :].unsqueeze(2), in_=gsc[:], axis=AX),
                wait=[("vS", q7)], inc=("vS", 1))
            q9 = sc.op("vector", lambda e: e.tensor_scalar_min(
                gf[:, 1, :], gf[:, 1, :], float(CAP * E)),
                wait=[("vS", q8)], inc=("vS", 1))
            qa = sc.op("vector", lambda e: e.tensor_copy(out=h2i[:], in_=gf[:, 1, :]),
                       wait=[("vS", q9)], inc=("vS", 1))
            qb = sc.op("vector", lambda e: e.scalar_tensor_tensor(
                out=rsc[:],
                in0=iota_e[:].unsqueeze(1).broadcast_to((128, NTT, E)),
                scalar=float(CAP), in1=pos_tok[:],
                op0=ALU.mult, op1=ALU.add), wait=[("vS", qa)], inc=("vS", 1))
            qc = sc.op("vector", lambda e: e.tensor_mul(gsc[:], rsc[:], gmask1[:]),
                       wait=[("vS", qb)], inc=("vS", 1))
            qd = sc.op("vector", lambda e: e.reduce_sum(
                out=gf[:, 0, :].unsqueeze(2), in_=gsc[:], axis=AX),
                wait=[("vS", qc)], inc=("vS", 1))
            qe = sc.op("vector", lambda e: e.tensor_scalar_min(
                gf[:, 0, :], gf[:, 0, :], float(CAP * E - 1)),
                wait=[("vS", qd)], inc=("vS", 1))
            qf = sc.op("vector", lambda e: e.tensor_copy(out=g1i[:], in_=gf[:, 0, :]),
                       wait=[("vS", qe)], inc=("vS", 1))
            qg = sc.op("vector", lambda e: e.tensor_mul(gsc[:], rsc[:], gmask2[:]),
                       wait=[("vS", qf)], inc=("vS", 1))
            qh = sc.op("vector", lambda e: e.reduce_sum(
                out=gf[:, 1, :].unsqueeze(2), in_=gsc[:], axis=AX),
                wait=[("vS", qg)], inc=("vS", 1))
            qi = sc.op("vector", lambda e: e.tensor_scalar_min(
                gf[:, 1, :], gf[:, 1, :], float(CAP * E - 1)),
                wait=[("vS", qh)], inc=("vS", 1))
            qj = sc.op("vector", lambda e: e.tensor_copy(out=g2i[:], in_=gf[:, 1, :]),
                       wait=[("vS", qi)], inc=("vS", 1))
            sc.flush_block()
        for tt in range(NTT):
            for hsrc in (h1i, h2i):
                sc.op("gpsimd", lambda e, tt=tt, hs=hsrc:
                      e.indirect_dma_start(
                    out=slots_d.ap().rearrange("(a b) -> a b", b=1),
                    out_offset=IndirectOffsetOnAxis(hs[:, tt:tt + 1], 0),
                    in_=iota_tok[:, tt:tt + 1], in_offset=None),
                    wait=([("vS", qj), ("dS", t_zs)] if tt == 0 and hsrc is h1i
                          else []),
                    inc=("dS", 16))

        if milestone == 1:
            sc.op("sync", lambda e: e.dma_start(
                out=dbg_g.ap().rearrange("p (k c) -> p k c", k=4)[:, 0, :],
                in_=g1i[:]), inc=("dX", 16))
            sc.op("sync", lambda e: e.dma_start(
                out=dbg_g.ap().rearrange("p (k c) -> p k c", k=4)[:, 1, :],
                in_=g2i[:]), inc=("dX", 16))
            sc.op("sync", lambda e: e.dma_start(
                out=dbg_g.ap().rearrange("p (k c) -> p k c", k=4)[:, 2, :],
                in_=h1i[:]), inc=("dX", 16))
            sc.op("sync", lambda e: e.dma_start(
                out=dbg_g.ap().rearrange("p (k c) -> p k c", k=4)[:, 3, :],
                in_=h2i[:]), inc=("dX", 16))
            sc.op("sync", lambda e: e.dma_start(
                out=dbg_cb.ap(), in_=comb[:].rearrange("p tt e -> p (tt e)")),
                inc=("dX", 16))
            sc.op("sync", wait=[("dX", sc.cnt["dX"]), ("dS", sc.cnt["dS"]),
                                ("dA", sc.cnt["dA"])])
            sc.flush_block()
            ctx_xnT.close()
            return nc

        with (
            nc.sbuf_tensor("cv_h0", [128, S], F32) as cv_h0,
            nc.sbuf_tensor("cv_h1", [128, S], F32) as cv_h1,
            nc.sbuf_tensor("cv_pre", [128, 2, S], F32) as cv_pre2,
            nc.sbuf_tensor("cv_attn", [128, 2, S], F32) as cv_attn2,
            nc.sbuf_tensor("cv_stage", [128, 2, 4, 128], F32) as cv_stage,
            nc.sbuf_tensor("xs_tile", [128, 2, 4, 128], F32) as xs_tile,
            nc.psum_tensor("ptr0", [128, 512], F32) as ptr0,
            nc.psum_tensor("ptr1", [128, 512], F32) as ptr1,
        ):
            ptr = (ptr0, ptr1)
            tok_stg_dma = {}
            tok_stg_cp = {}
            tok_xs = {}
            t_attn_prev_tr = 0
            t_attn_prev = 0
            t_pre_prev = 0
            t_si_toks = {}
            for dc in range(NDC):
                q0 = q_t[:, 2 * dc:2 * dc + 1]
                q1 = q_t[:, 2 * dc + 1:2 * dc + 2]
                c0 = c_t[:, 2 * dc:2 * dc + 1]
                c1 = c_t[:, 2 * dc + 1:2 * dc + 2]
                om = prm_omega[:, dc:dc + 1]
                cv_pre = cv_pre2[:, dc % 2, :]
                cv_attn = cv_attn2[:, dc % 2, :]
                t_s0 = sc.op("vector", lambda e, dc=dc, q0=q0: e.tensor_tensor_scan(
                    out=cv_h0[:], data0=q0.broadcast_to((128, S)),
                    data1=xnT[:, dc, :], initial=0.0, op0=ALU.mult, op1=ALU.add),
                    wait=[("sS", t_si_toks.get(dc - 2, 0)), ("vS", t_pre_prev)]
                    + (a2_dma_done if dc == 0 else []),
                    inc=("vS", 1))
                t_s1 = sc.op("vector", lambda e, dc=dc, q1=q1: e.tensor_tensor_scan(
                    out=cv_h1[:], data0=q1.broadcast_to((128, S)),
                    data1=xnT[:, dc, :], initial=0.0, op0=ALU.mult, op1=ALU.add),
                    inc=("vS", 1))
                t_m = sc.op("vector", lambda e, c0=c0, cv_pre=cv_pre:
                            e.tensor_scalar_mul(
                    cv_pre[:], cv_h0[:], c0), wait=[("vS", t_s0)], inc=("vS", 1))
                t_st1 = sc.op("vector", lambda e, c1=c1, cv_pre=cv_pre:
                              e.scalar_tensor_tensor(
                    out=cv_pre[:], in0=cv_h1[:], scalar=c1, in1=cv_pre[:],
                    op0=ALU.mult, op1=ALU.add),
                    wait=[("vS", max(t_s1, t_m))], inc=("vS", 1))
                t_pre = sc.op("vector", lambda e, dc=dc, om=om, cv_pre=cv_pre:
                              e.scalar_tensor_tensor(
                    out=cv_pre[:], in0=xnT[:, dc, :], scalar=om, in1=cv_pre[:],
                    op0=ALU.mult, op1=ALU.add), wait=[("vS", t_st1)], inc=("vS", 1))
                t_pre_prev = t_pre
                t_attn = sc.op("scalar", lambda e, cv_attn=cv_attn, cv_pre=cv_pre:
                               e.activation(
                    out=cv_attn[:], in_=cv_pre[:], func=AF.Silu),
                    wait=[("vS", t_pre), ("tS", t_attn_prev_tr)], inc=("sS", 1))
                t_si_toks[dc] = t_attn
                for g in range(8):
                    pb = g % 2
                    sb = g % 2
                    gid = dc * 8 + g
                    t_xs = sc.op("sync", lambda e, dc=dc, g=g, sb=sb: e.dma_start(
                        out=xs_tile[:, sb, :, :],
                        in_=x_stage[g][:, :, dc * 128:(dc + 1) * 128]),
                        wait=[("vS", tok_stg_cp.get(gid - 2, 0))],
                        inc=("x%d" % sb, 16))
                    wt = [("sS", t_attn)]
                    if gid >= 2:
                        wt.append(("vS", tok_stg_cp[gid - 2]))
                    t_tr = None
                    for j in range(4):
                        t_tr = sc.op("tensor", lambda e, pb=pb, g=g, j=j,
                                     cv_attn=cv_attn: e.transpose(
                            ptr[pb][:, j * 128:(j + 1) * 128],
                            cv_attn[:, (g * 4 + j) * 128:(g * 4 + j + 1) * 128],
                            id_f32[:]), wait=(wt if j == 0 else []), inc=("tS", 1))
                    t_attn_prev_tr = t_tr
                    wcp = [("tS", t_tr), ("x%d" % sb, t_xs)]
                    if gid >= 2:
                        wcp.append(tok_stg_dma[gid - 2])
                    tcp = sc.op("vector", lambda e, pb=pb, sb=sb: e.tensor_tensor(
                        out=cv_stage[:, sb, :, :],
                        in0=xs_tile[:, sb, :, :],
                        in1=ptr[pb][:].rearrange("p (j t) -> p j t", t=128),
                        op=ALU.add),
                        wait=wcp, inc=("vS", 1))
                    tok_stg_cp[gid] = tcp
                    sts = "st%d" % (gid % 2)
                    tok_stg_dma[gid] = (sts, sc.op(
                        "scalar", lambda e, dc=dc, g=g, sb=sb: e.dma_start(
                            out=out_stage[g][:, :, dc * 128:(dc + 1) * 128],
                            in_=cv_stage[:, sb, :, :]),
                        wait=[("vS", tcp)], inc=(sts, 16)))
            sc.flush_block()
        t_conv_done = [("st0", sc.cnt["st0"]), ("st1", sc.cnt["st1"])]
        ctx_xnT.close()

        with (
            nc.sbuf_tensor("slots_sb", [128, NJ, E], I32) as slots_sb,
            nc.sbuf_tensor("xc", [128, 2, NJ, AUGW], BF16) as xc,
            nc.sbuf_tensor("xnc", [128, NDC, CAP], BF16) as xnc,
            nc.sbuf_tensor("h_act", [128, NHQ, CAP], BF16) as h_act,
            nc.sbuf_tensor("t_sil", [128, 2, CAP], F32) as t_sil,
            nc.sbuf_tensor("y_s", [128, NJ, D], F32) as y_s,
            nc.sbuf_tensor("cf32", [128, 2, NJ, 2], F32) as cf32,
            nc.sbuf_tensor("w1b", [128, 2, NDC, 512], BF16) as w1b,
            nc.sbuf_tensor("w3b", [128, 2, NDC, 512], BF16) as w3b,
            nc.sbuf_tensor("w2b", [128, 2, 4, D], BF16) as w2b,
            nc.psum_tensor("PH1", [128, 2048], F32) as PH1,
            nc.psum_tensor("PH3", [128, 1536], F32) as PH3,
            nc.psum_tensor("PT", [128, D], BF16) as PT,
        ):
            aug_done = [("st0", sc.cnt["st0"]), ("st1", sc.cnt["st1"]),
                        ("dA", sc.cnt["dA"])]
            t_sl = sc.op("sync", lambda e: e.dma_start(
                out=slots_sb[:], in_=slots_sbv[:, 0:NJ, :]),
                wait=[("dS", sc.cnt["dS"])] + t_conv_done, inc=("xs", 16))

            RY = [[], [], []]
            F3 = []
            FPT = []
            w_free = {"w1": [list(t_conv_done), list(t_conv_done)],
                      "w3": [list(t_conv_done), list(t_conv_done)],
                      "w2": [list(t_conv_done), list(t_conv_done)]}
            tsil_free = [[], []]
            hq_ctr = 0
            w2h_ctr = 0
            dma_tok = {"w1": {}, "w3": {}, "w2": {}}
            WSEM = {("w1", 0): "w10", ("w1", 1): "w11", ("w3", 0): "w30",
                    ("w3", 1): "w31", ("w2", 0): "w20", ("w2", 1): "w21"}

            def emit_wdma(mat, idx, src_slice, dst_buf, bufsel):
                toks = w_free[mat][bufsel]
                sname = WSEM[(mat, bufsel)]
                t = sc.op("gpsimd", lambda e, s=src_slice, d=dst_buf: e.dma_start(
                    out=d, in_=s), wait=toks, inc=(sname, 16))
                w_free[mat][bufsel] = []
                dma_tok[mat][idx] = (sname, t)

            gat_tok = {}
            tr_tok = {}
            tr_pe_tok = {}
            exp_evac_tok = {}
            dbg_block = []
            ys_tok = {}
            y_dma_last = 0

            for ei in range(E):
                gsem = "dG%d" % (ei % 2)
                wgath = [("xs", t_sl)] + (aug_done if ei == 0 else [])
                if ei >= 2:
                    wgath.append(("tS", tr_pe_tok[ei - 2]))
                    wgath.append(("vS", exp_evac_tok[ei - 2]))
                tg = None
                for j in range(NJ):
                    tg = sc.op("gpsimd", lambda e, ei=ei, j=j: e.indirect_dma_start(
                        out=xc[:, ei % 2, j, :], out_offset=None,
                        in_=aug_d.ap(),
                        in_offset=IndirectOffsetOnAxis(
                            slots_sb[:, j, ei:ei + 1], 0)),
                        wait=(wgath if j == 0 else []), inc=(gsem, 16))
                gat_tok[ei] = tg
                t_cf = sc.op("vector", lambda e, ei=ei: e.tensor_copy(
                    out=cf32[:, ei % 2, :, :],
                    in_=xc[:, ei % 2, :, D + 2 * ei:D + 2 * ei + 2]),
                    wait=[(gsem, tg)], inc=("vS", 1))

                t_dbg = None
                if milestone == 2 and ei == 0:
                    t_dbg = sc.op("sync", lambda e: e.dma_start(
                        out=dbg_xc.ap(), in_=xc[:, 0, :, :]),
                        wait=[(gsem, tg)], inc=("dX", 16))

                tprev = 0
                for j in range(NJ):
                    wtr = ([(gsem, tg)] + dbg_block) if j == 0 else []
                    if j == 0 and ei >= 1:
                        wtr.append(("vS", tr_tok[ei - 1]))
                    ttr = None
                    for dc in range(NDC):
                        ttr = sc.op("tensor", lambda e, ei=ei, j=j, dc=dc: e.transpose(
                            PT[:, dc * 128:(dc + 1) * 128],
                            xc[:, ei % 2, j, dc * 128:(dc + 1) * 128], id_bf[:]),
                            wait=(wtr + FPT if dc == 0 else []), inc=("tS", 1))
                    FPT = []
                    tr_pe_tok[ei] = ttr
                    tcp = sc.op("vector", lambda e, j=j: e.tensor_copy(
                        out=xnc[:, :, j * 128:(j + 1) * 128],
                        in_=PT[:].rearrange("p (dc t) -> p dc t", t=128)),
                        wait=[("tS", ttr)], inc=("vS", 1))
                    FPT = [("vS", tcp)]
                    tprev = tcp
                tr_tok[ei] = tprev
                if milestone == 2 and ei == 0:
                    t_dbg2 = sc.op("sync", lambda e: e.dma_start(
                        out=dbg_xnc.ap(), in_=xnc[:]),
                        wait=[("vS", tprev)], inc=("dX", 16))
                    dbg_block.append(("dX", t_dbg2))

                for q in range(4):
                    for jj in (0, 1):
                        emit_wdma("w1", hq_ctr,
                                  w1_v[ei][:, :, (q * 2 + jj) * 512:(q * 2 + jj + 1) * 512],
                                  w1b[:, hq_ctr % 2, :, :], hq_ctr % 2)
                        emit_wdma("w3", hq_ctr,
                                  w3_v[ei][:, :, (q * 2 + jj) * 512:(q * 2 + jj + 1) * 512],
                                  w3b[:, hq_ctr % 2, :, :], hq_ctr % 2)
                        hq_ctr += 1
                        emit_wdma("w2", w2h_ctr,
                                  w2_v[ei][:, q * 8 + jj * 4:q * 8 + (jj + 1) * 4, :],
                                  w2b[:, w2h_ctr % 2, :, :], w2h_ctr % 2)
                        w2h_ctr += 1
                    hact_toks = {}
                    for hp in range(NHQ):
                        hpg = q * NHQ + hp
                        hqi = (ei * 4 + q) * 2 + hp // 4
                        col = (hp % 4) * 128
                        w1tok = dma_tok["w1"][hqi]
                        w3tok = dma_tok["w3"][hqi]
                        g1 = None
                        wts = [w1tok] + RY[0] + RY[1]
                        if hpg == 0:
                            wts.append(("vS", tr_tok[ei]))
                        for dc in range(NDC):
                            for ts in range(3):
                                cw = 512 if ts < 2 else CAP - 1024
                                g1 = sc.op("tensor", lambda e, dc=dc, ts=ts, hqi=hqi,
                                           col=col, cw=cw: e.matmul(
                                    PH1[:, ts * 512:ts * 512 + cw],
                                    w1b[:, hqi % 2, dc, col:col + 128],
                                    xnc[:, dc, ts * 512:ts * 512 + cw],
                                    start=(dc == 0), stop=(dc == NDC - 1)),
                                    wait=(wts if (dc == 0 and ts == 0) else []),
                                    inc=(("tS", 1) if (dc == NDC - 1 and ts == 2)
                                         else None))
                        RY[0] = []
                        RY[1] = []
                        w_free["w1"][hqi % 2].append(("tS", g1))
                        g2 = None
                        wts3 = [w3tok] + F3 + RY[2]
                        for dc in range(NDC):
                            for ts in range(3):
                                cw = 512 if ts < 2 else CAP - 1024
                                g2 = sc.op("tensor", lambda e, dc=dc, ts=ts, hqi=hqi,
                                           col=col, cw=cw: e.matmul(
                                    PH3[:, ts * 512:ts * 512 + cw],
                                    w3b[:, hqi % 2, dc, col:col + 128],
                                    xnc[:, dc, ts * 512:ts * 512 + cw],
                                    start=(dc == 0), stop=(dc == NDC - 1)),
                                    wait=(wts3 if (dc == 0 and ts == 0) else []),
                                    inc=(("tS", 1) if (dc == NDC - 1 and ts == 2)
                                         else None))
                        F3 = []
                        RY[2] = []
                        w_free["w3"][hqi % 2].append(("tS", g2))
                        sb = hpg % 2
                        t_si = sc.op("scalar", lambda e, sb=sb: e.activation(
                            out=t_sil[:, sb, :], in_=PH1[:, 0:CAP], func=AF.Silu),
                            wait=[("tS", g1)] + tsil_free[sb], inc=("sS", 1))
                        tsil_free[sb] = []
                        RY[0].append(("sS", t_si))
                        RY[1].append(("sS", t_si))
                        wev_h = [("sS", t_si), ("tS", g2)]
                        if milestone == 2 and ei == 0 and q == 1 and hp == 0:
                            wev_h += dbg_block
                        t_ev = sc.op("vector", lambda e, sb=sb, hp=hp: e.tensor_tensor(
                            out=h_act[:, hp, :], in0=t_sil[:, sb, :],
                            in1=PH3[:, 0:CAP], op=ALU.mult),
                            wait=wev_h, inc=("vS", 1))
                        F3.append(("vS", t_ev))
                        tsil_free[sb].append(("vS", t_ev))
                        hact_toks[hp] = t_ev
                    if milestone == 2 and ei == 0 and q == 0:
                        t_dbg3 = sc.op("sync", lambda e: e.dma_start(
                            out=dbg_h.ap(), in_=h_act[:]),
                            wait=[("vS", hact_toks[NHQ - 1])], inc=("dX", 16))
                        dbg_block.append(("dX", t_dbg3))

                    w2t0 = dma_tok["w2"][(ei * 4 + q) * 2]
                    w2t1 = dma_tok["w2"][(ei * 4 + q) * 2 + 1]
                    w2s0 = ((ei * 4 + q) * 2) % 2
                    w2s1 = ((ei * 4 + q) * 2 + 1) % 2
                    for j in range(NJ):
                        reg = j % 3
                        ybase = (PH1, reg * 1024) if reg < 2 else (PH3, 0)
                        gy = None
                        for hc in range(NHQ):
                            for ds in (0, 1):
                                w = []
                                if j == 0 and ds == 0:
                                    w.append(("vS", hact_toks[hc]))
                                if hc == 0 and ds == 0:
                                    w += [w2t0, w2t1] + RY[reg]
                                    if j > 0:
                                        w.append(("vS", hact_toks[NHQ - 1]))
                                w2s = w2s0 if hc < 4 else w2s1
                                gy = sc.op("tensor", lambda e, hc=hc, ds=ds, j=j,
                                           yb=ybase, w2s=w2s: e.matmul(
                                    yb[0][:, yb[1] + ds * 512:
                                          yb[1] + (ds + 1) * 512],
                                    h_act[:, hc, j * 128:(j + 1) * 128],
                                    w2b[:, w2s, hc % 4, ds * 512:(ds + 1) * 512],
                                    start=(hc == 0), stop=(hc == NHQ - 1)),
                                    wait=w,
                                    inc=(("tS", 1) if (hc == NHQ - 1 and ds == 1)
                                         else None))
                        RY[reg] = []
                        last_y_mm = gy
                        if j >= NJ - 2:
                            w_free["w2"][w2s0].append(("tS", gy))
                            w_free["w2"][w2s1].append(("tS", gy))
                        chi_f = cf32[:, ei % 2, j, 0:1]
                        clo_f = cf32[:, ei % 2, j, 1:2]
                        wev = [("tS", gy)]
                        if q == 0 and j == 0:
                            wev.append(("vS", t_cf))
                        if q == 0 and j == 0 and ei > 0:
                            wev.append(("dY", y_dma_last))
                        if q > 0:
                            wev.append(("vS", ys_tok[j]))
                        if q == 0:
                            e1a = sc.op("vector", lambda e, j=j, yb=ybase, cs=chi_f:
                                        e.tensor_scalar_mul(
                                y_s[:, j, :], yb[0][:, yb[1]:yb[1] + 1024],
                                cs), wait=wev, inc=("vS", 1))
                        else:
                            e1a = sc.op("vector", lambda e, j=j, yb=ybase, cs=chi_f:
                                        e.scalar_tensor_tensor(
                                out=y_s[:, j, :],
                                in0=yb[0][:, yb[1]:yb[1] + 1024],
                                scalar=cs, in1=y_s[:, j, :],
                                op0=ALU.mult, op1=ALU.add), wait=wev, inc=("vS", 1))
                        e1 = sc.op("vector", lambda e, j=j, yb=ybase, cs=clo_f:
                                   e.scalar_tensor_tensor(
                            out=y_s[:, j, :],
                            in0=yb[0][:, yb[1]:yb[1] + 1024],
                            scalar=cs, in1=y_s[:, j, :],
                            op0=ALU.mult, op1=ALU.add),
                            wait=[("vS", e1a)], inc=("vS", 1))
                        RY[reg].append(("vS", e1))
                        ys_tok[j] = e1
                        exp_evac_tok[ei] = e1
                        if q == 3:
                            y_dma_last = sc.op("scalar", lambda e, ei=ei, j=j:
                                               e.dma_start(
                                out=yall_rows[ei * NJ + j], in_=y_s[:, j, :]),
                                wait=[("vS", e1)], inc=("dY", 16))
            sc.flush_block()

        NB = 4
        with (
            nc.sbuf_tensor("base", [128, NB, D], F32) as basef,
            nc.sbuf_tensor("yg1", [128, NB, D], F32) as yg1,
            nc.sbuf_tensor("yg2", [128, NB, D], F32) as yg2,
            nc.sbuf_tensor("osum", [128, NB, D], F32) as osum,
        ):
            y_all_done = sc.cnt["dY"]
            tok_add = {}
            tok_st = {}
            for tt in range(NTT):
                b = tt % NB
                wb = t_conv_done + [("vS", tok_add.get(tt - NB, 0))]
                t_b = sc.op("sync", lambda e, tt=tt, b=b: e.dma_start(
                    out=basef[:, b, :], in_=out_rows[tt]),
                    wait=wb, inc=("fb%d" % b, 16))
                wg1 = [("dY", y_all_done), ("vS", tok_add.get(tt - NB, 0))]
                t_g1 = sc.op("gpsimd", lambda e, tt=tt, b=b: e.indirect_dma_start(
                    out=yg1[:, b, :], out_offset=None,
                    in_=yall_d.ap(),
                    in_offset=IndirectOffsetOnAxis(g1i[:, tt:tt + 1], 0)),
                    wait=wg1, inc=("fg%d" % b, 16))
                t_g2 = sc.op("gpsimd", lambda e, tt=tt, b=b: e.indirect_dma_start(
                    out=yg2[:, b, :], out_offset=None,
                    in_=yall_d.ap(),
                    in_offset=IndirectOffsetOnAxis(g2i[:, tt:tt + 1], 0)),
                    inc=("fg%d" % b, 16))
                wadd = [("fb%d" % b, t_b), ("fg%d" % b, t_g2)]
                if tt >= NB:
                    wadd.append(tok_st[tt - NB])
                ta = sc.op("vector", lambda e, b=b: e.tensor_add(
                    osum[:, b, :], basef[:, b, :], yg1[:, b, :]), wait=wadd,
                    inc=("vS", 1))
                tok_add[tt] = sc.op("vector", lambda e, b=b: e.tensor_add(
                    osum[:, b, :], osum[:, b, :], yg2[:, b, :]),
                    wait=[("vS", ta)], inc=("vS", 1))
                stn = "fs%d" % b
                tok_st[tt] = (stn, sc.op("scalar", lambda e, tt=tt, b=b: e.dma_start(
                    out=out_rows[tt], in_=osum[:, b, :]),
                    wait=[("vS", tok_add[tt])], inc=(stn, 16)))
            sc.op("sync", wait=[("fs%d" % b, sc.cnt["fs%d" % b])
                                for b in range(NB)])
            sc.flush_block()

    return nc


def _layout_inputs(inputs):
    x = np.ascontiguousarray(np.asarray(inputs["L0_attn_x"], dtype=np.float32))
    w1T = np.ascontiguousarray(
        np.asarray(inputs["L0_ffn_w1"], dtype=np.float32).transpose(0, 2, 1))
    w3T = np.ascontiguousarray(
        np.asarray(inputs["L0_ffn_w3"], dtype=np.float32).transpose(0, 2, 1))
    w2T = np.ascontiguousarray(
        np.asarray(inputs["L0_ffn_w2"], dtype=np.float32).transpose(0, 2, 1))
    gwT = np.ascontiguousarray(
        np.asarray(inputs["L0_ffn_gate_w"], dtype=np.float32).T)

    def prm2(a):
        a = np.asarray(a, dtype=np.float32).reshape(D, 2)
        return np.ascontiguousarray(a.reshape(NDC, 128, 2).transpose(1, 0, 2)
                                    .reshape(128, 2 * NDC))

    delta_t = prm2(inputs["L0_attn_delta"])
    alpha_t = prm2(inputs["L0_attn_alpha"])
    beta_t = prm2(inputs["L0_attn_beta"])
    gamma_t = prm2(inputs["L0_attn_gamma"])
    omega_t = np.ascontiguousarray(
        np.asarray(inputs["L0_attn_omega"], dtype=np.float32).reshape(NDC, 128).T)
    shared = dict(w1t=w1T, w3t=w3T, w2t=w2T, gwt=gwT, delta_t=delta_t,
                  alpha_t=alpha_t, beta_t=beta_t, gamma_t=gamma_t, omega_t=omega_t)
    in_maps = [dict(shared, x=np.ascontiguousarray(x[b])) for b in range(8)]
    return in_maps


def kernel(**inputs):
    in_maps = _layout_inputs(inputs)
    nc = build_nc(milestone=3)
    res = run_bass_kernel_spmd(nc, in_maps, core_ids=list(range(8)))
    return np.stack([np.asarray(r["out"]) for r in res.results], axis=0)


if __name__ == "__main__":
    import sys
    import time
    ms = int(sys.argv[1]) if len(sys.argv) > 1 else 3
    t0 = time.time()
    nc = build_nc(milestone=ms)
    n = sum(1 for _ in nc.all_instructions())
    print(f"milestone {ms}: built in {time.time()-t0:.1f}s, {n} instructions")


# version 23 (final)
# speedup vs baseline: 1.0155x; 1.0155x over previous
import numpy as np
import concourse.bass as bass
import concourse.mybir as mybir
from concourse.bass_utils import run_bass_kernel_spmd
from concourse.bass_core_rust import IndirectOffsetOnAxis

F32 = mybir.dt.float32
BF16 = mybir.dt.bfloat16
I32 = mybir.dt.int32
AF = mybir.ActivationFunctionType
ALU = mybir.AluOpType
AX = mybir.AxisListType.X

S = 4096
D = 1024
H = 4096
E = 8
NDC = D // 128
NTT = S // 128
CAP = 1280
NJ = CAP // 128
AUGW = D + 2 * E
NSLOT = 1024 * 11
NHQ = H // 128 // 4
SCALE = float(np.sqrt(0.5))
EPS = 1e-5
BIG = 1e30

ENGINES = ("sync", "gpsimd", "vector", "scalar", "tensor")


class Sched:

    def __init__(self, nc, semnames):
        self.nc = nc
        self.q = {e: [] for e in ENGINES}
        self.cnt = {s: 0 for s in semnames}
        self.sem = {}

    def op(self, eng, build=None, wait=(), inc=None):
        w = tuple((s, v) for (s, v) in wait if v and v > 0)
        tok = None
        if inc is not None:
            self.cnt[inc[0]] += inc[1]
            tok = self.cnt[inc[0]]

        def closure(e, _w=w, _b=build, _i=inc):
            for s, v in _w:
                e.wait_ge(self.sem[s], v)
            if _b is not None:
                ins = _b(e)
                if _i is not None:
                    ins.then_inc(self.sem[_i[0]], _i[1])
            elif _i is not None:
                e.nop().then_inc(self.sem[_i[0]], _i[1])

        self.q[eng].append(closure)
        return tok

    def flush_block(self):
        with self.nc.Block() as block:
            for e in ENGINES:
                if self.q[e]:
                    ops = self.q[e]
                    self.q[e] = []
                    getattr(block, e)(lambda eng, ops=ops: [f(eng) for f in ops])


def emit_identity(sc, identity):
    sq = identity.shape[0]
    tm = sc.op("gpsimd", lambda e: e.memset(identity, 0.0), inc=("dW", 1))
    sc.op("gpsimd", lambda e: e.affine_select(
        out=identity, in_=identity, compare_op=ALU.not_equal,
        fill=1.0, base=0, pattern=[[-1, sq]], channel_multiplier=1),
        wait=[("dW", tm)])


def build_nc(milestone=3):
    nc = bass.Bass("TRN2", target_bir_lowering=False)

    x_d = nc.dram_tensor("x", [S, D], F32, kind="ExternalInput")
    w1_d = nc.dram_tensor("w1t", [E, D, H], F32, kind="ExternalInput")
    w3_d = nc.dram_tensor("w3t", [E, D, H], F32, kind="ExternalInput")
    w2_d = nc.dram_tensor("w2t", [E, H, D], F32, kind="ExternalInput")
    gw_d = nc.dram_tensor("gwt", [D, E], F32, kind="ExternalInput")
    delta_d = nc.dram_tensor("delta_t", [128, 2 * NDC], F32, kind="ExternalInput")
    alpha_d = nc.dram_tensor("alpha_t", [128, 2 * NDC], F32, kind="ExternalInput")
    beta_d = nc.dram_tensor("beta_t", [128, 2 * NDC], F32, kind="ExternalInput")
    gamma_d = nc.dram_tensor("gamma_t", [128, 2 * NDC], F32, kind="ExternalInput")
    omega_d = nc.dram_tensor("omega_t", [128, NDC], F32, kind="ExternalInput")
    out_d = nc.dram_tensor("out", [S, D], F32, kind="ExternalOutput")
    aug_d = nc.dram_tensor("xn_aug", [S, AUGW], BF16, kind="Internal")
    slots_d = nc.dram_tensor(
        "slots", [NSLOT], I32,
        kind="ExternalOutput" if milestone in (1, 2) else "Internal")
    yall_d = nc.dram_tensor("yall", [E * CAP, D], F32,
                            kind="ExternalOutput" if milestone == 2 else "Internal")
    if milestone == 2:
        dbg_xc = nc.dram_tensor("dbg_xc", [128, NJ, AUGW], BF16, kind="ExternalOutput")
        dbg_xnc = nc.dram_tensor("dbg_xnc", [128, NDC, CAP], BF16,
                                 kind="ExternalOutput")
        dbg_h = nc.dram_tensor("dbg_h", [128, NHQ, CAP], BF16, kind="ExternalOutput")
    if milestone == 1:
        dbg_g = nc.dram_tensor("dbg_g", [128, 4 * NTT], I32, kind="ExternalOutput")
        dbg_cb = nc.dram_tensor("dbg_cb", [128, NTT * E], F32, kind="ExternalOutput")

    x_tiles = x_d.ap().rearrange("(tt p) d -> tt p d", p=128)
    out_rows = out_d.ap().rearrange("(tt p) d -> tt p d", p=128)
    out_stage = out_d.ap().rearrange("(tg tts p) d -> tg p tts d", tts=4, p=128)
    x_stage = x_d.ap().rearrange("(tg tts p) d -> tg p tts d", tts=4, p=128)
    aug_rows = aug_d.ap().rearrange("(tt p) c -> tt p c", p=128)
    gw_v = gw_d.ap().rearrange("(dc p) e -> p dc e", p=128)
    w1_v = w1_d.ap().rearrange("e (dc p) h -> e p dc h", p=128)
    w3_v = w3_d.ap().rearrange("e (dc p) h -> e p dc h", p=128)
    w2_v = w2_d.ap().rearrange("e (hc p) d -> e p hc d", p=128)
    yall_rows = yall_d.ap().rearrange("(r p) d -> r p d", p=128)
    slots_v = slots_d.ap().rearrange("(sl e) -> sl e", e=E)
    slots_sbv = slots_d.ap().rearrange("(jj p e) -> p jj e", p=128, e=E)

    sems = ["dX", "dW", "dO", "tS", "vS", "sS",
            "x0", "x1", "st0", "st1", "xs", "dA", "dS", "dG0", "dG1", "dY",
            "w10", "w11", "w30", "w31", "w20", "w21",
            "fb0", "fb1", "fb2", "fb3", "fg0", "fg1", "fg2", "fg3",
            "fs0", "fs1", "fs2", "fs3"]
    sc = Sched(nc, sems)

    import contextlib
    ctx = contextlib.ExitStack()
    with ctx:
        for s in sems:
            sc.sem[s] = ctx.enter_context(nc.semaphore(s))

        s_col = ctx.enter_context(nc.sbuf_tensor("s_col", [128, NTT], F32))
        ss_col = ctx.enter_context(nc.sbuf_tensor("ss_col", [128, NTT], F32))
        mu_col = ctx.enter_context(nc.sbuf_tensor("mu_col", [128, NTT], F32))
        rstd_col = ctx.enter_context(nc.sbuf_tensor("rstd_col", [128, NTT], F32))
        var_col = ctx.enter_context(nc.sbuf_tensor("var_col", [128, NTT], F32))
        scores = ctx.enter_context(nc.sbuf_tensor("scores", [128, NTT, E], F32))
        comb = ctx.enter_context(nc.sbuf_tensor("comb", [128, NTT, E], F32))
        gmask1 = ctx.enter_context(nc.sbuf_tensor("gmask1", [128, NTT, E], F32))
        gmask2 = ctx.enter_context(nc.sbuf_tensor("gmask2", [128, NTT, E], F32))
        gsc = ctx.enter_context(nc.sbuf_tensor("gsc", [128, NTT, E], F32))
        gms = ctx.enter_context(nc.sbuf_tensor("gms", [128, NTT, E], F32))
        gm1 = ctx.enter_context(nc.sbuf_tensor("gm1", [128, NTT, 1], F32))
        gm2 = ctx.enter_context(nc.sbuf_tensor("gm2", [128, NTT, 1], F32))
        gw1g = ctx.enter_context(nc.sbuf_tensor("gw1g", [128, NTT, 1], F32))
        gw2g = ctx.enter_context(nc.sbuf_tensor("gw2g", [128, NTT, 1], F32))
        comb_hl = ctx.enter_context(nc.sbuf_tensor("comb_hl", [128, NTT, E, 2], BF16))
        gwf = ctx.enter_context(nc.sbuf_tensor("gwf", [128, NDC, E], F32))
        gwhi = ctx.enter_context(nc.sbuf_tensor("gwhi", [128, NDC, E], BF16))
        gwlo = ctx.enter_context(nc.sbuf_tensor("gwlo", [128, NDC, E], BF16))
        prm_delta = ctx.enter_context(nc.sbuf_tensor("prm_delta", [128, 2 * NDC], F32))
        prm_alpha = ctx.enter_context(nc.sbuf_tensor("prm_alpha", [128, 2 * NDC], F32))
        prm_beta = ctx.enter_context(nc.sbuf_tensor("prm_beta", [128, 2 * NDC], F32))
        prm_gamma = ctx.enter_context(nc.sbuf_tensor("prm_gamma", [128, 2 * NDC], F32))
        prm_omega = ctx.enter_context(nc.sbuf_tensor("prm_omega", [128, NDC], F32))
        q_t = ctx.enter_context(nc.sbuf_tensor("q_t", [128, 2 * NDC], F32))
        c_t = ctx.enter_context(nc.sbuf_tensor("c_t", [128, 2 * NDC], F32))
        tmp_p = ctx.enter_context(nc.sbuf_tensor("tmp_p", [128, 2 * NDC], F32))
        id_bf = ctx.enter_context(nc.sbuf_tensor("id_bf", [128, 128], BF16))
        id_f32 = ctx.enter_context(nc.sbuf_tensor("id_f32", [128, 128], F32))
        epsb = ctx.enter_context(nc.sbuf_tensor("epsb", [128, 1], F32))
        iota_e = ctx.enter_context(nc.sbuf_tensor("iota_e", [128, E], F32))
        iota_tok = ctx.enter_context(nc.sbuf_tensor("iota_tok", [128, NTT], I32))
        zslot = ctx.enter_context(nc.sbuf_tensor("zslot", [128, NSLOT // 128], I32))
        pos_tok = ctx.enter_context(nc.sbuf_tensor("pos_tok", [128, NTT, E], F32))
        g1i = ctx.enter_context(nc.sbuf_tensor("g1i", [128, NTT], I32))
        g2i = ctx.enter_context(nc.sbuf_tensor("g2i", [128, NTT], I32))
        h1i = ctx.enter_context(nc.sbuf_tensor("h1i", [128, NTT], I32))
        h2i = ctx.enter_context(nc.sbuf_tensor("h2i", [128, NTT], I32))
        ctx_xnT = contextlib.ExitStack()
        xnT = ctx_xnT.enter_context(nc.sbuf_tensor("xnT", [128, NDC, S], BF16))

        t_prm = None
        for (src, dst) in ((delta_d.ap(), prm_delta), (alpha_d.ap(), prm_alpha),
                           (beta_d.ap(), prm_beta), (gamma_d.ap(), prm_gamma),
                           (omega_d.ap(), prm_omega), (gw_v, gwf)):
            t_prm = sc.op("sync", lambda e, s=src, d=dst: e.dma_start(out=d[:], in_=s),
                          inc=("dX", 16))
        emit_identity(sc, id_bf[:])
        emit_identity(sc, id_f32[:])
        sc.op("gpsimd", lambda e: e.memset(epsb[:], EPS), inc=("dW", 1))
        sc.op("gpsimd", lambda e: e.iota(
            iota_e[:], [[1, E]], channel_multiplier=0,
            allow_small_or_imprecise_dtypes=True), inc=("dW", 1))
        sc.op("gpsimd", lambda e: e.iota(iota_tok[:], [[128, NTT]],
                                         channel_multiplier=1), inc=("dW", 1))
        t_ident = sc.op("gpsimd", lambda e: e.memset(zslot[:], 0), inc=("dW", 1))
        t_zs = sc.op("gpsimd", lambda e: e.dma_start(
            out=slots_d.ap().rearrange("(p c) -> p c", p=128), in_=zslot[:]),
            wait=[("dW", t_ident)], inc=("dS", 16))

        sc.op("scalar", lambda e: e.activation(out=q_t[:], in_=prm_delta[:],
                                               func=AF.Sigmoid), wait=[("dX", t_prm)])
        t_sig = sc.op("scalar", lambda e: e.activation(
            out=tmp_p[:], in_=prm_alpha[:], func=AF.Sigmoid), inc=("sS", 1))
        v1 = sc.op("vector", lambda e: e.tensor_mul(tmp_p[:], q_t[:], tmp_p[:]),
                   wait=[("sS", t_sig)], inc=("vS", 1))
        v2 = sc.op("vector", lambda e: e.tensor_mul(c_t[:], q_t[:], prm_beta[:]),
                   inc=("vS", 1))
        sc.op("vector", lambda e: e.tensor_scalar(
            out=q_t[:], in0=tmp_p[:], scalar1=-1.0, scalar2=1.0,
            op0=ALU.mult, op1=ALU.add), wait=[("vS", v2)])
        v4 = sc.op("vector", lambda e: e.tensor_mul(c_t[:], c_t[:], prm_gamma[:]),
                   wait=[("vS", v2)], inc=("vS", 1))
        sc.op("vector", lambda e: e.tensor_scalar_mul(c_t[:], c_t[:], SCALE),
              wait=[("vS", v4)])
        v6 = sc.op("vector", lambda e: e.tensor_copy(out=gwhi[:], in_=gwf[:]),
                   inc=("vS", 1))
        sc.op("vector", lambda e: e.tensor_sub(gwlo[:], gwf[:], gwhi[:]),
              wait=[("vS", v6)])
        sc.flush_block()

        with (
            nc.sbuf_tensor("x_tile", [128, 2, D], F32) as x_tile,
            nc.sbuf_tensor("sqdump", [128, 2, D], F32) as sqdump,
        ):
            with (
                nc.sbuf_tensor("xnf", [128, D], F32) as xnf,
                nc.sbuf_tensor("aug_sb", [128, 2, D], BF16) as aug_sb,
                nc.sbuf_tensor("xn_lo", [128, 2, D], BF16) as xn_lo,
                nc.sbuf_tensor("xnlo_t", [128, 2, D], BF16) as xnlo_t,
                nc.psum_tensor("pst_hi", [128, D], BF16) as pst_hi,
                nc.psum_tensor("pst_lo", [128, D], BF16) as pst_lo,
                nc.psum_tensor("psc0", [128, E], F32) as psc0,
                nc.psum_tensor("psc1", [128, E], F32) as psc1,
            ):
                psc = (psc0, psc1)
                tok_tT = {}
                tok_cp = {}
                tok_scp = {}
                aug_tok = {}
                t_rs_prev = 0
                for tt in range(NTT):
                    b = tt % 2
                    xsn = "x%d" % b
                    t_dma = sc.op("sync", lambda e, tt=tt, b=b: e.dma_start(
                        out=x_tile[:, b, :], in_=x_tiles[tt]),
                        wait=[("vS", tok_cp.get(tt - 2, 0)),
                              ("sS", t_rs_prev)], inc=(xsn, 16))
                    ml = (tt, tt + 1)
                    t_red = sc.op("vector", lambda e, tt=tt, b=b: e.reduce_sum(
                        out=s_col[:, tt:tt + 1], in_=x_tile[:, b, :], axis=AX),
                        wait=[(xsn, t_dma)], inc=("vS", 1))
                    t_sq = sc.op("scalar", lambda e, tt=tt, b=b: e.activation(
                        out=sqdump[:, b, :], in_=x_tile[:, b, :], func=AF.Square,
                        accum_out=ss_col[:, tt:tt + 1]),
                        wait=[(xsn, t_dma), ("sS", t_rs_prev)], inc=("sS", 1))
                    v_mu = sc.op("vector", lambda e, tt=tt: e.tensor_scalar_mul(
                        mu_col[:, tt:tt + 1], s_col[:, tt:tt + 1], 1.0 / D),
                        wait=[("vS", t_red)], inc=("vS", 1))
                    v_ms = sc.op("vector", lambda e, tt=tt: e.tensor_scalar_mul(
                        var_col[:, tt:tt + 1], ss_col[:, tt:tt + 1], 1.0 / D),
                        wait=[("sS", t_sq)], inc=("vS", 1))
                    v_msq = sc.op("vector", lambda e, tt=tt: e.tensor_mul(
                        s_col[:, tt:tt + 1], mu_col[:, tt:tt + 1],
                        mu_col[:, tt:tt + 1]),
                        wait=[("vS", v_mu)], inc=("vS", 1))
                    v_var = sc.op("vector", lambda e, tt=tt: e.tensor_sub(
                        var_col[:, tt:tt + 1], var_col[:, tt:tt + 1],
                        s_col[:, tt:tt + 1]),
                        wait=[("vS", v_msq)], inc=("vS", 1))
                    s_ln = sc.op("scalar", lambda e, tt=tt: e.activation(
                        out=var_col[:, tt:tt + 1], in_=var_col[:, tt:tt + 1],
                        func=AF.Ln, bias=epsb[:, 0:1]),
                        wait=[("vS", v_var), ("dW", t_ident)], inc=("sS", 1))
                    t_rs = sc.op("scalar", lambda e, tt=tt: e.activation(
                        out=rstd_col[:, tt:tt + 1], in_=var_col[:, tt:tt + 1],
                        func=AF.Exp, scale=-0.5),
                        wait=[("sS", s_ln)], inc=("sS", 1))
                    t_rs_prev = t_rs
                    wv = [("sS", t_rs)]
                    if tt >= 2:
                        wv.append(("tS", tok_tT[tt - 2]))
                        wv.append(aug_tok[tt - 2])
                    t_xnf = sc.op("vector", lambda e, tt=tt, b=b: e.tensor_scalar(
                        out=xnf[:], in0=x_tile[:, b, :],
                        scalar1=mu_col[:, tt:tt + 1], scalar2=rstd_col[:, tt:tt + 1],
                        op0=ALU.subtract, op1=ALU.mult), wait=wv, inc=("vS", 1))
                    t_hic = sc.op("vector", lambda e, b=b: e.tensor_copy(
                        out=aug_sb[:, b, :], in_=xnf[:]),
                        wait=[("vS", t_xnf)], inc=("vS", 1))
                    t_hilo = sc.op("vector", lambda e, b=b: e.tensor_sub(
                        xn_lo[:, b, :], xnf[:], aug_sb[:, b, :]),
                        wait=[("vS", t_hic)], inc=("vS", 1))
                    aug_tok[tt] = ("st%d" % b, sc.op(
                        "sync", lambda e, tt=tt, b=b: e.dma_start(
                            out=aug_rows[tt][:, 0:D], in_=aug_sb[:, b, :]),
                        wait=[("vS", t_hilo)], inc=("st%d" % b, 16)))
                    wt = [("vS", t_hilo), ("dW", t_ident)]
                    if tt >= 1:
                        wt.append(("vS", tok_cp[tt - 1]))
                    for dc in range(NDC):
                        tok_tT[tt] = sc.op("tensor", lambda e, dc=dc, b=b: e.transpose(
                            pst_hi[:, dc * 128:(dc + 1) * 128],
                            aug_sb[:, b, dc * 128:(dc + 1) * 128], id_bf[:]),
                            wait=(wt if dc == 0 else []), inc=("tS", 1))
                    for dc in range(NDC):
                        tok_tT[tt] = sc.op("tensor", lambda e, dc=dc, b=b: e.transpose(
                            pst_lo[:, dc * 128:(dc + 1) * 128],
                            xn_lo[:, b, dc * 128:(dc + 1) * 128], id_bf[:]),
                            inc=("tS", 1))
                    sc.op("vector", lambda e, tt=tt: e.tensor_copy(
                        out=xnT[:, :, tt * 128:(tt + 1) * 128],
                        in_=pst_hi[:].rearrange("p (dc t) -> p dc t", t=128)),
                        wait=[("tS", tok_tT[tt])])
                    tok_cp[tt] = sc.op("vector", lambda e, b=b: e.tensor_copy(
                        out=xnlo_t[:, b, :], in_=pst_lo[:]), inc=("vS", 1))
                    wg = [("vS", tok_cp[tt])]
                    if tt >= 2:
                        wg.append(("vS", tok_scp[tt - 2]))
                    t_gmm = None
                    n_gmm = 3 * NDC
                    gi = 0
                    for dc in range(NDC):
                        hi_ap = xnT[:, dc, tt * 128:(tt + 1) * 128]
                        lo_ap = xnlo_t[:, b, dc * 128:(dc + 1) * 128]
                        for (lhs, rhs) in ((hi_ap, gwhi), (hi_ap, gwlo), (lo_ap, gwhi)):
                            t_gmm = sc.op("tensor", lambda e, l=lhs, r=rhs, dc=dc,
                                          gi=gi, pt=psc[tt % 2]: e.matmul(
                                pt[:], l, r[:, dc, :], start=(gi == 0),
                                stop=(gi == n_gmm - 1)),
                                wait=(wg if gi == 0 else []), inc=("tS", 1))
                            gi += 1
                    tok_scp[tt] = sc.op("vector", lambda e, tt=tt, pt=psc[tt % 2]:
                                        e.tensor_copy(out=scores[:, tt, :], in_=pt[:]),
                                        wait=[("tS", t_gmm)], inc=("vS", 1))
                sc.flush_block()

        g1t = sc.op("vector", lambda e: e.reduce_max(out=gm1[:], in_=scores[:], axis=AX),
                    inc=("vS", 1))
        g2t = sc.op("vector", lambda e: e.tensor_tensor(
            out=gmask1[:], in0=scores[:],
            in1=gm1[:].broadcast_to((128, NTT, E)), op=ALU.is_equal),
            wait=[("vS", g1t)], inc=("vS", 1))
        g3t = sc.op("vector", lambda e: e.scalar_tensor_tensor(
            out=gsc[:], in0=gmask1[:], scalar=-BIG, in1=scores[:],
            op0=ALU.mult, op1=ALU.add), wait=[("vS", g2t)], inc=("vS", 1))
        g4t = sc.op("vector", lambda e: e.reduce_max(out=gm2[:], in_=gsc[:], axis=AX),
                    wait=[("vS", g3t)], inc=("vS", 1))
        g5t = sc.op("vector", lambda e: e.tensor_tensor(
            out=gmask2[:], in0=gsc[:],
            in1=gm2[:].broadcast_to((128, NTT, E)), op=ALU.is_equal),
            wait=[("vS", g4t)], inc=("vS", 1))
        g5u = sc.op("vector", lambda e: e.tensor_add(gms[:], gmask1[:], gmask2[:]),
                    wait=[("vS", g5t)], inc=("vS", 1))
        t_d12 = sc.op("vector", lambda e: e.tensor_tensor(
            out=gw1g[:], in0=gm1[:], in1=gm2[:], op=ALU.subtract),
            wait=[("vS", g4t)], inc=("vS", 1))
        t_sg = sc.op("scalar", lambda e: e.activation(
            out=gw1g[:], in_=gw1g[:], func=AF.Sigmoid),
            wait=[("vS", t_d12)], inc=("sS", 1))
        g6t = sc.op("vector", lambda e: e.tensor_scalar(
            out=gw2g[:], in0=gw1g[:], scalar1=-1.0, scalar2=1.0,
            op0=ALU.mult, op1=ALU.add), wait=[("sS", t_sg)], inc=("vS", 1))
        g7t = sc.op("vector", lambda e: e.tensor_tensor(
            out=gsc[:], in0=gmask1[:],
            in1=gw1g[:].broadcast_to((128, NTT, E)), op=ALU.mult),
            wait=[("sS", t_sg)], inc=("vS", 1))
        g8t = sc.op("vector", lambda e: e.tensor_tensor(
            out=comb[:], in0=gmask2[:],
            in1=gw2g[:].broadcast_to((128, NTT, E)), op=ALU.mult),
            wait=[("vS", max(g6t, g5t))], inc=("vS", 1))
        g9t = sc.op("vector", lambda e: e.tensor_add(comb[:], comb[:], gsc[:]),
                    wait=[("vS", max(g8t, g7t))], inc=("vS", 1))
        gat = sc.op("vector", lambda e: e.tensor_copy(
            out=comb_hl[:, :, :, 0], in_=comb[:]),
            wait=[("vS", g9t)], inc=("vS", 1))
        gbt = sc.op("vector", lambda e: e.tensor_sub(
            comb_hl[:, :, :, 1], comb[:], comb_hl[:, :, :, 0]),
            wait=[("vS", gat)], inc=("vS", 1))
        t_combdma = sc.op("sync", lambda e: e.dma_start(
            out=aug_d.ap().rearrange("(tt p) c -> p tt c", p=128)[:, :, D:AUGW],
            in_=comb_hl[:].rearrange("p tt e hl -> p tt (e hl)")),
            wait=[("vS", gbt)], inc=("dA", 16))
        sc.flush_block()

        a2_dma_done = [("st0", sc.cnt["st0"]), ("st1", sc.cnt["st1"]),
                       ("dA", sc.cnt["dA"])]
        with (
            nc.sbuf_tensor("combT", [E, S], F32) as combT,
            nc.sbuf_tensor("maskT", [E, S], F32) as maskT,
            nc.sbuf_tensor("posT", [E, S], F32) as posT,
            nc.sbuf_tensor("ones8", [E, 1], F32) as ones8,
            nc.sbuf_tensor("gf", [128, 2, NTT], F32) as gf,
            nc.sbuf_tensor("rsc", [128, NTT, E], F32) as rsc,
            nc.psum_tensor("prt0", [E, 128], F32) as prt0,
            nc.psum_tensor("prt1", [E, 128], F32) as prt1,
            nc.psum_tensor("prb0", [128, E], F32) as prb0,
            nc.psum_tensor("prb1", [128, E], F32) as prb1,
        ):
            prt = (prt0, prt1)
            prb = (prb0, prb1)
            t_on8 = sc.op("gpsimd", lambda e: e.memset(ones8[:], 1.0),
                          wait=a2_dma_done, inc=("dW", 1))
            tok_ct = {}
            tok_cc = {}
            for tt in range(NTT):
                tok_ct[tt] = sc.op("tensor", lambda e, tt=tt, pt=prt[tt % 2]:
                                   e.matmul(
                    pt[:], gms[:, tt, :], id_f32[:], start=True, stop=True),
                    wait=[("vS", tok_cc.get(tt - 2, 0) if tt >= 2 else g5u)],
                    inc=("tS", 1))
                tok_cc[tt] = sc.op("vector", lambda e, tt=tt, pt=prt[tt % 2]:
                                   e.tensor_copy(
                    out=combT[:, tt * 128:(tt + 1) * 128], in_=pt[:]),
                    wait=[("tS", tok_ct[tt])] + (a2_dma_done if tt == 0 else []),
                    inc=("vS", 1))
            r1 = sc.op("vector", lambda e: e.tensor_single_scalar(
                out=maskT[:], in_=combT[:], scalar=0.5, op=ALU.is_gt),
                wait=[("vS", tok_cc[NTT - 1])], inc=("vS", 1))
            r2 = sc.op("vector", lambda e: e.tensor_tensor_scan(
                out=posT[:], data0=ones8[:, 0:1].broadcast_to((E, S)),
                data1=maskT[:], initial=0.0, op0=ALU.mult, op1=ALU.add),
                wait=[("vS", r1), ("dW", t_on8)], inc=("vS", 1))
            tok_bt = {}
            tok_bc = {}
            for tt in range(NTT):
                tok_bt[tt] = sc.op("tensor", lambda e, tt=tt, pb=prb[tt % 2]:
                                   e.matmul(
                    pb[:], posT[:, tt * 128:(tt + 1) * 128],
                    id_f32[0:E, 0:E], start=True, stop=True),
                    wait=[("vS", r2 if tt < 2 else tok_bc[tt - 2])], inc=("tS", 1))
                tok_bc[tt] = sc.op("vector", lambda e, tt=tt, pb=prb[tt % 2]:
                                   e.tensor_copy(
                    out=pos_tok[:, tt, :], in_=pb[:]),
                    wait=[("tS", tok_bt[tt])], inc=("vS", 1))
            q1 = sc.op("vector", lambda e: e.tensor_scalar_add(
                pos_tok[:], pos_tok[:], -1.0),
                wait=[("vS", tok_bc[NTT - 1])], inc=("vS", 1))
            q2 = sc.op("vector", lambda e: e.scalar_tensor_tensor(
                out=rsc[:], in0=pos_tok[:], scalar=float(E),
                in1=iota_e[:].unsqueeze(1).broadcast_to((128, NTT, E)),
                op0=ALU.mult, op1=ALU.add), wait=[("vS", q1)], inc=("vS", 1))
            q3 = sc.op("vector", lambda e: e.tensor_mul(gsc[:], rsc[:], gmask1[:]),
                       wait=[("vS", q2)], inc=("vS", 1))
            q4 = sc.op("vector", lambda e: e.reduce_sum(
                out=gf[:, 0, :].unsqueeze(2), in_=gsc[:], axis=AX),
                wait=[("vS", q3)], inc=("vS", 1))
            q5 = sc.op("vector", lambda e: e.tensor_scalar_min(
                gf[:, 0, :], gf[:, 0, :], float(CAP * E)),
                wait=[("vS", q4)], inc=("vS", 1))
            q6 = sc.op("vector", lambda e: e.tensor_copy(out=h1i[:], in_=gf[:, 0, :]),
                       wait=[("vS", q5)], inc=("vS", 1))
            q7 = sc.op("vector", lambda e: e.tensor_mul(gsc[:], rsc[:], gmask2[:]),
                       wait=[("vS", q6)], inc=("vS", 1))
            q8 = sc.op("vector", lambda e: e.reduce_sum(
                out=gf[:, 1, :].unsqueeze(2), in_=gsc[:], axis=AX),
                wait=[("vS", q7)], inc=("vS", 1))
            q9 = sc.op("vector", lambda e: e.tensor_scalar_min(
                gf[:, 1, :], gf[:, 1, :], float(CAP * E)),
                wait=[("vS", q8)], inc=("vS", 1))
            qa = sc.op("vector", lambda e: e.tensor_copy(out=h2i[:], in_=gf[:, 1, :]),
                       wait=[("vS", q9)], inc=("vS", 1))
            qb = sc.op("vector", lambda e: e.scalar_tensor_tensor(
                out=rsc[:],
                in0=iota_e[:].unsqueeze(1).broadcast_to((128, NTT, E)),
                scalar=float(CAP), in1=pos_tok[:],
                op0=ALU.mult, op1=ALU.add), wait=[("vS", qa)], inc=("vS", 1))
            qc = sc.op("vector", lambda e: e.tensor_mul(gsc[:], rsc[:], gmask1[:]),
                       wait=[("vS", qb)], inc=("vS", 1))
            qd = sc.op("vector", lambda e: e.reduce_sum(
                out=gf[:, 0, :].unsqueeze(2), in_=gsc[:], axis=AX),
                wait=[("vS", qc)], inc=("vS", 1))
            qe = sc.op("vector", lambda e: e.tensor_scalar_min(
                gf[:, 0, :], gf[:, 0, :], float(CAP * E - 1)),
                wait=[("vS", qd)], inc=("vS", 1))
            qf = sc.op("vector", lambda e: e.tensor_copy(out=g1i[:], in_=gf[:, 0, :]),
                       wait=[("vS", qe)], inc=("vS", 1))
            qg = sc.op("vector", lambda e: e.tensor_mul(gsc[:], rsc[:], gmask2[:]),
                       wait=[("vS", qf)], inc=("vS", 1))
            qh = sc.op("vector", lambda e: e.reduce_sum(
                out=gf[:, 1, :].unsqueeze(2), in_=gsc[:], axis=AX),
                wait=[("vS", qg)], inc=("vS", 1))
            qi = sc.op("vector", lambda e: e.tensor_scalar_min(
                gf[:, 1, :], gf[:, 1, :], float(CAP * E - 1)),
                wait=[("vS", qh)], inc=("vS", 1))
            qj = sc.op("vector", lambda e: e.tensor_copy(out=g2i[:], in_=gf[:, 1, :]),
                       wait=[("vS", qi)], inc=("vS", 1))
            sc.flush_block()
        for tt in range(NTT):
            for hsrc in (h1i, h2i):
                sc.op("gpsimd", lambda e, tt=tt, hs=hsrc:
                      e.indirect_dma_start(
                    out=slots_d.ap().rearrange("(a b) -> a b", b=1),
                    out_offset=IndirectOffsetOnAxis(hs[:, tt:tt + 1], 0),
                    in_=iota_tok[:, tt:tt + 1], in_offset=None),
                    wait=([("vS", qj), ("dS", t_zs)] if tt == 0 and hsrc is h1i
                          else []),
                    inc=("dS", 16))

        if milestone == 1:
            sc.op("sync", lambda e: e.dma_start(
                out=dbg_g.ap().rearrange("p (k c) -> p k c", k=4)[:, 0, :],
                in_=g1i[:]), inc=("dX", 16))
            sc.op("sync", lambda e: e.dma_start(
                out=dbg_g.ap().rearrange("p (k c) -> p k c", k=4)[:, 1, :],
                in_=g2i[:]), inc=("dX", 16))
            sc.op("sync", lambda e: e.dma_start(
                out=dbg_g.ap().rearrange("p (k c) -> p k c", k=4)[:, 2, :],
                in_=h1i[:]), inc=("dX", 16))
            sc.op("sync", lambda e: e.dma_start(
                out=dbg_g.ap().rearrange("p (k c) -> p k c", k=4)[:, 3, :],
                in_=h2i[:]), inc=("dX", 16))
            sc.op("sync", lambda e: e.dma_start(
                out=dbg_cb.ap(), in_=comb[:].rearrange("p tt e -> p (tt e)")),
                inc=("dX", 16))
            sc.op("sync", wait=[("dX", sc.cnt["dX"]), ("dS", sc.cnt["dS"]),
                                ("dA", sc.cnt["dA"])])
            sc.flush_block()
            ctx_xnT.close()
            return nc

        with (
            nc.sbuf_tensor("cv_h0", [128, S], F32) as cv_h0,
            nc.sbuf_tensor("cv_h1", [128, S], F32) as cv_h1,
            nc.sbuf_tensor("cv_pre", [128, 2, S], F32) as cv_pre2,
            nc.sbuf_tensor("cv_attn", [128, 2, S], F32) as cv_attn2,
            nc.sbuf_tensor("cv_stage", [128, 2, 4, 128], F32) as cv_stage,
            nc.sbuf_tensor("xs_tile", [128, 2, 4, 128], F32) as xs_tile,
            nc.psum_tensor("ptr0", [128, 512], F32) as ptr0,
            nc.psum_tensor("ptr1", [128, 512], F32) as ptr1,
        ):
            ptr = (ptr0, ptr1)
            tok_stg_dma = {}
            tok_stg_cp = {}
            tok_stg_mm = {}
            tok_xs = {}
            t_attn_prev_tr = 0
            t_attn_prev = 0
            t_pre_prev = 0
            t_m_prev = 0
            t_si_toks = {}
            for dc in range(NDC):
                q0 = q_t[:, 2 * dc:2 * dc + 1]
                q1 = q_t[:, 2 * dc + 1:2 * dc + 2]
                c0 = c_t[:, 2 * dc:2 * dc + 1]
                c1 = c_t[:, 2 * dc + 1:2 * dc + 2]
                om = prm_omega[:, dc:dc + 1]
                cv_pre = cv_pre2[:, dc % 2, :]
                cv_attn = cv_attn2[:, dc % 2, :]
                t_s0 = sc.op("vector", lambda e, dc=dc, q0=q0: e.tensor_tensor_scan(
                    out=cv_h0[:], data0=q0.broadcast_to((128, S)),
                    data1=xnT[:, dc, :], initial=0.0, op0=ALU.mult, op1=ALU.add),
                    wait=[("sS", max(t_m_prev, t_si_toks.get(dc - 2, 0))),
                          ("vS", t_pre_prev)]
                    + (a2_dma_done if dc == 0 else []),
                    inc=("vS", 1))
                t_s1 = sc.op("vector", lambda e, dc=dc, q1=q1: e.tensor_tensor_scan(
                    out=cv_h1[:], data0=q1.broadcast_to((128, S)),
                    data1=xnT[:, dc, :], initial=0.0, op0=ALU.mult, op1=ALU.add),
                    inc=("vS", 1))
                t_m = sc.op("scalar", lambda e, c0=c0, cv_pre=cv_pre:
                            e.activation(
                    out=cv_pre[:], in_=cv_h0[:], func=AF.Copy, scale=c0),
                    wait=[("vS", t_s0)], inc=("sS", 1))
                t_m_prev = t_m
                t_st1 = sc.op("vector", lambda e, c1=c1, cv_pre=cv_pre:
                              e.scalar_tensor_tensor(
                    out=cv_pre[:], in0=cv_h1[:], scalar=c1, in1=cv_pre[:],
                    op0=ALU.mult, op1=ALU.add),
                    wait=[("vS", t_s1), ("sS", t_m)], inc=("vS", 1))
                t_pre = sc.op("vector", lambda e, dc=dc, om=om, cv_pre=cv_pre:
                              e.scalar_tensor_tensor(
                    out=cv_pre[:], in0=xnT[:, dc, :], scalar=om, in1=cv_pre[:],
                    op0=ALU.mult, op1=ALU.add), wait=[("vS", t_st1)], inc=("vS", 1))
                t_pre_prev = t_pre
                t_attn = sc.op("scalar", lambda e, cv_attn=cv_attn, cv_pre=cv_pre:
                               e.activation(
                    out=cv_attn[:], in_=cv_pre[:], func=AF.Silu),
                    wait=[("vS", t_pre), ("tS", t_attn_prev_tr)], inc=("sS", 1))
                t_si_toks[dc] = t_attn
                for g in range(8):
                    pb = g % 2
                    sb = g % 2
                    gid = dc * 8 + g
                    t_xs = sc.op("sync", lambda e, dc=dc, g=g, sb=sb: e.dma_start(
                        out=xs_tile[:, sb, :, :],
                        in_=x_stage[g][:, :, dc * 128:(dc + 1) * 128]),
                        wait=[("tS", tok_stg_mm.get(gid - 2, 0))],
                        inc=("x%d" % sb, 16))
                    wt = [("sS", max(t_attn, tok_stg_cp.get(gid - 2, 0))),
                          ("x%d" % sb, t_xs)]
                    t_tr = None
                    for j in range(4):
                        sc.op("tensor", lambda e, pb=pb, g=g, j=j,
                              cv_attn=cv_attn: e.matmul(
                            ptr[pb][:, j * 128:(j + 1) * 128],
                            cv_attn[:, (g * 4 + j) * 128:(g * 4 + j + 1) * 128],
                            id_f32[:], start=True, stop=False),
                            wait=(wt if j == 0 else []))
                        t_tr = sc.op("tensor", lambda e, pb=pb, sb=sb, j=j:
                                     e.matmul(
                            ptr[pb][:, j * 128:(j + 1) * 128], id_f32[:],
                            xs_tile[:, sb, j, :], start=False, stop=True),
                            inc=("tS", 1))
                    t_attn_prev_tr = t_tr
                    tok_stg_mm[gid] = t_tr
                    wcp = [("tS", t_tr)]
                    if gid >= 2:
                        wcp.append(tok_stg_dma[gid - 2])
                    tcp = sc.op("scalar", lambda e, pb=pb, sb=sb: e.activation(
                        out=cv_stage[:, sb, :, :],
                        in_=ptr[pb][:].rearrange("p (j t) -> p j t", t=128),
                        func=AF.Copy),
                        wait=wcp, inc=("sS", 1))
                    tok_stg_cp[gid] = tcp
                    sts = "st%d" % (gid % 2)
                    tok_stg_dma[gid] = (sts, sc.op(
                        "scalar", lambda e, dc=dc, g=g, sb=sb: e.dma_start(
                            out=out_stage[g][:, :, dc * 128:(dc + 1) * 128],
                            in_=cv_stage[:, sb, :, :]),
                        wait=[("sS", tcp)], inc=(sts, 16)))
            sc.flush_block()
        t_conv_done = [("st0", sc.cnt["st0"]), ("st1", sc.cnt["st1"])]
        ctx_xnT.close()

        with (
            nc.sbuf_tensor("slots_sb", [128, NJ, E], I32) as slots_sb,
            nc.sbuf_tensor("xc", [128, 2, NJ, AUGW], BF16) as xc,
            nc.sbuf_tensor("xnc", [128, NDC, CAP], BF16) as xnc,
            nc.sbuf_tensor("h_act", [128, NHQ, CAP], BF16) as h_act,
            nc.sbuf_tensor("t_sil", [128, 2, CAP], F32) as t_sil,
            nc.sbuf_tensor("y_s", [128, NJ, D], F32) as y_s,
            nc.sbuf_tensor("cf32", [128, 2, NJ, 2], F32) as cf32,
            nc.sbuf_tensor("w1b", [128, 2, NDC, 512], BF16) as w1b,
            nc.sbuf_tensor("w3b", [128, 2, NDC, 512], BF16) as w3b,
            nc.sbuf_tensor("w2b", [128, 2, 4, D], BF16) as w2b,
            nc.psum_tensor("PH1", [128, 2048], F32) as PH1,
            nc.psum_tensor("PH3", [128, 1536], F32) as PH3,
            nc.psum_tensor("PT", [128, D], BF16) as PT,
        ):
            aug_done = [("st0", sc.cnt["st0"]), ("st1", sc.cnt["st1"]),
                        ("dA", sc.cnt["dA"])]
            t_sl = sc.op("sync", lambda e: e.dma_start(
                out=slots_sb[:], in_=slots_sbv[:, 0:NJ, :]),
                wait=[("dS", sc.cnt["dS"])] + t_conv_done, inc=("xs", 16))

            RY = [[], [], []]
            F3 = []
            FPT = []
            w_free = {"w1": [list(t_conv_done), list(t_conv_done)],
                      "w3": [list(t_conv_done), list(t_conv_done)],
                      "w2": [list(t_conv_done), list(t_conv_done)]}
            tsil_free = [[], []]
            hq_ctr = 0
            w2h_ctr = 0
            dma_tok = {"w1": {}, "w3": {}, "w2": {}}
            WSEM = {("w1", 0): "w10", ("w1", 1): "w11", ("w3", 0): "w30",
                    ("w3", 1): "w31", ("w2", 0): "w20", ("w2", 1): "w21"}

            def emit_wdma(mat, idx, src_slice, dst_buf, bufsel):
                toks = w_free[mat][bufsel]
                sname = WSEM[(mat, bufsel)]
                t = sc.op("gpsimd", lambda e, s=src_slice, d=dst_buf: e.dma_start(
                    out=d, in_=s), wait=toks, inc=(sname, 16))
                w_free[mat][bufsel] = []
                dma_tok[mat][idx] = (sname, t)

            gat_tok = {}
            tr_tok = {}
            tr_pe_tok = {}
            exp_evac_tok = {}
            dbg_block = []
            ys_tok = {}
            y_dma_last = 0

            for ei in range(E):
                gsem = "dG%d" % (ei % 2)
                wgath = [("xs", t_sl)] + (aug_done if ei == 0 else [])
                if ei >= 2:
                    wgath.append(("tS", tr_pe_tok[ei - 2]))
                    wgath.append(("vS", exp_evac_tok[ei - 2]))
                tg = None
                for j in range(NJ):
                    tg = sc.op("gpsimd", lambda e, ei=ei, j=j: e.indirect_dma_start(
                        out=xc[:, ei % 2, j, :], out_offset=None,
                        in_=aug_d.ap(),
                        in_offset=IndirectOffsetOnAxis(
                            slots_sb[:, j, ei:ei + 1], 0)),
                        wait=(wgath if j == 0 else []), inc=(gsem, 16))
                gat_tok[ei] = tg
                t_cf = sc.op("vector", lambda e, ei=ei: e.tensor_copy(
                    out=cf32[:, ei % 2, :, :],
                    in_=xc[:, ei % 2, :, D + 2 * ei:D + 2 * ei + 2]),
                    wait=[(gsem, tg)], inc=("vS", 1))

                t_dbg = None
                if milestone == 2 and ei == 0:
                    t_dbg = sc.op("sync", lambda e: e.dma_start(
                        out=dbg_xc.ap(), in_=xc[:, 0, :, :]),
                        wait=[(gsem, tg)], inc=("dX", 16))

                tprev = 0
                for j in range(NJ):
                    wtr = ([(gsem, tg)] + dbg_block) if j == 0 else []
                    if j == 0 and ei >= 1:
                        wtr.append(("vS", tr_tok[ei - 1]))
                    ttr = None
                    for dc in range(NDC):
                        ttr = sc.op("tensor", lambda e, ei=ei, j=j, dc=dc: e.transpose(
                            PT[:, dc * 128:(dc + 1) * 128],
                            xc[:, ei % 2, j, dc * 128:(dc + 1) * 128], id_bf[:]),
                            wait=(wtr + FPT if dc == 0 else []), inc=("tS", 1))
                    FPT = []
                    tr_pe_tok[ei] = ttr
                    tcp = sc.op("vector", lambda e, j=j: e.tensor_copy(
                        out=xnc[:, :, j * 128:(j + 1) * 128],
                        in_=PT[:].rearrange("p (dc t) -> p dc t", t=128)),
                        wait=[("tS", ttr)], inc=("vS", 1))
                    FPT = [("vS", tcp)]
                    tprev = tcp
                tr_tok[ei] = tprev
                if milestone == 2 and ei == 0:
                    t_dbg2 = sc.op("sync", lambda e: e.dma_start(
                        out=dbg_xnc.ap(), in_=xnc[:]),
                        wait=[("vS", tprev)], inc=("dX", 16))
                    dbg_block.append(("dX", t_dbg2))

                for q in range(4):
                    for jj in (0, 1):
                        emit_wdma("w1", hq_ctr,
                                  w1_v[ei][:, :, (q * 2 + jj) * 512:(q * 2 + jj + 1) * 512],
                                  w1b[:, hq_ctr % 2, :, :], hq_ctr % 2)
                        emit_wdma("w3", hq_ctr,
                                  w3_v[ei][:, :, (q * 2 + jj) * 512:(q * 2 + jj + 1) * 512],
                                  w3b[:, hq_ctr % 2, :, :], hq_ctr % 2)
                        hq_ctr += 1
                        emit_wdma("w2", w2h_ctr,
                                  w2_v[ei][:, q * 8 + jj * 4:q * 8 + (jj + 1) * 4, :],
                                  w2b[:, w2h_ctr % 2, :, :], w2h_ctr % 2)
                        w2h_ctr += 1
                    hact_toks = {}
                    for hp in range(NHQ):
                        hpg = q * NHQ + hp
                        hqi = (ei * 4 + q) * 2 + hp // 4
                        col = (hp % 4) * 128
                        w1tok = dma_tok["w1"][hqi]
                        w3tok = dma_tok["w3"][hqi]
                        g1 = None
                        wts = [w1tok] + RY[0] + RY[1]
                        if hpg == 0:
                            wts.append(("vS", tr_tok[ei]))
                        for dc in range(NDC):
                            for ts in range(3):
                                cw = 512 if ts < 2 else CAP - 1024
                                g1 = sc.op("tensor", lambda e, dc=dc, ts=ts, hqi=hqi,
                                           col=col, cw=cw: e.matmul(
                                    PH1[:, ts * 512:ts * 512 + cw],
                                    w1b[:, hqi % 2, dc, col:col + 128],
                                    xnc[:, dc, ts * 512:ts * 512 + cw],
                                    start=(dc == 0), stop=(dc == NDC - 1)),
                                    wait=(wts if (dc == 0 and ts == 0) else []),
                                    inc=(("tS", 1) if (dc == NDC - 1 and ts == 2)
                                         else None))
                        RY[0] = []
                        RY[1] = []
                        w_free["w1"][hqi % 2].append(("tS", g1))
                        g2 = None
                        wts3 = [w3tok] + F3 + RY[2]
                        for dc in range(NDC):
                            for ts in range(3):
                                cw = 512 if ts < 2 else CAP - 1024
                                g2 = sc.op("tensor", lambda e, dc=dc, ts=ts, hqi=hqi,
                                           col=col, cw=cw: e.matmul(
                                    PH3[:, ts * 512:ts * 512 + cw],
                                    w3b[:, hqi % 2, dc, col:col + 128],
                                    xnc[:, dc, ts * 512:ts * 512 + cw],
                                    start=(dc == 0), stop=(dc == NDC - 1)),
                                    wait=(wts3 if (dc == 0 and ts == 0) else []),
                                    inc=(("tS", 1) if (dc == NDC - 1 and ts == 2)
                                         else None))
                        F3 = []
                        RY[2] = []
                        w_free["w3"][hqi % 2].append(("tS", g2))
                        sb = hpg % 2
                        t_si = sc.op("scalar", lambda e, sb=sb: e.activation(
                            out=t_sil[:, sb, :], in_=PH1[:, 0:CAP], func=AF.Silu),
                            wait=[("tS", g1)] + tsil_free[sb], inc=("sS", 1))
                        tsil_free[sb] = []
                        RY[0].append(("sS", t_si))
                        RY[1].append(("sS", t_si))
                        wev_h = [("sS", t_si), ("tS", g2)]
                        if milestone == 2 and ei == 0 and q == 1 and hp == 0:
                            wev_h += dbg_block
                        t_ev = sc.op("vector", lambda e, sb=sb, hp=hp: e.tensor_tensor(
                            out=h_act[:, hp, :], in0=t_sil[:, sb, :],
                            in1=PH3[:, 0:CAP], op=ALU.mult),
                            wait=wev_h, inc=("vS", 1))
                        F3.append(("vS", t_ev))
                        tsil_free[sb].append(("vS", t_ev))
                        hact_toks[hp] = t_ev
                    if milestone == 2 and ei == 0 and q == 0:
                        t_dbg3 = sc.op("sync", lambda e: e.dma_start(
                            out=dbg_h.ap(), in_=h_act[:]),
                            wait=[("vS", hact_toks[NHQ - 1])], inc=("dX", 16))
                        dbg_block.append(("dX", t_dbg3))

                    w2t0 = dma_tok["w2"][(ei * 4 + q) * 2]
                    w2t1 = dma_tok["w2"][(ei * 4 + q) * 2 + 1]
                    w2s0 = ((ei * 4 + q) * 2) % 2
                    w2s1 = ((ei * 4 + q) * 2 + 1) % 2
                    for j in range(NJ):
                        reg = j % 3
                        ybase = (PH1, reg * 1024) if reg < 2 else (PH3, 0)
                        gy = None
                        for hc in range(NHQ):
                            for ds in (0, 1):
                                w = []
                                if j == 0 and ds == 0:
                                    w.append(("vS", hact_toks[hc]))
                                if hc == 0 and ds == 0:
                                    w += [w2t0, w2t1] + RY[reg]
                                    if j > 0:
                                        w.append(("vS", hact_toks[NHQ - 1]))
                                w2s = w2s0 if hc < 4 else w2s1
                                gy = sc.op("tensor", lambda e, hc=hc, ds=ds, j=j,
                                           yb=ybase, w2s=w2s: e.matmul(
                                    yb[0][:, yb[1] + ds * 512:
                                          yb[1] + (ds + 1) * 512],
                                    h_act[:, hc, j * 128:(j + 1) * 128],
                                    w2b[:, w2s, hc % 4, ds * 512:(ds + 1) * 512],
                                    start=(hc == 0), stop=(hc == NHQ - 1)),
                                    wait=w,
                                    inc=(("tS", 1) if (hc == NHQ - 1 and ds == 1)
                                         else None))
                        RY[reg] = []
                        last_y_mm = gy
                        if j >= NJ - 2:
                            w_free["w2"][w2s0].append(("tS", gy))
                            w_free["w2"][w2s1].append(("tS", gy))
                        chi_f = cf32[:, ei % 2, j, 0:1]
                        clo_f = cf32[:, ei % 2, j, 1:2]
                        wev = [("tS", gy)]
                        if q == 0 and j == 0:
                            wev.append(("vS", t_cf))
                        if q == 0 and j == 0 and ei > 0:
                            wev.append(("dY", y_dma_last))
                        if q > 0:
                            wev.append(("vS", ys_tok[j]))
                        if q == 0:
                            e1a = sc.op("vector", lambda e, j=j, yb=ybase, cs=chi_f:
                                        e.tensor_scalar_mul(
                                y_s[:, j, :], yb[0][:, yb[1]:yb[1] + 1024],
                                cs), wait=wev, inc=("vS", 1))
                        else:
                            e1a = sc.op("vector", lambda e, j=j, yb=ybase, cs=chi_f:
                                        e.scalar_tensor_tensor(
                                out=y_s[:, j, :],
                                in0=yb[0][:, yb[1]:yb[1] + 1024],
                                scalar=cs, in1=y_s[:, j, :],
                                op0=ALU.mult, op1=ALU.add), wait=wev, inc=("vS", 1))
                        e1 = sc.op("vector", lambda e, j=j, yb=ybase, cs=clo_f:
                                   e.scalar_tensor_tensor(
                            out=y_s[:, j, :],
                            in0=yb[0][:, yb[1]:yb[1] + 1024],
                            scalar=cs, in1=y_s[:, j, :],
                            op0=ALU.mult, op1=ALU.add),
                            wait=[("vS", e1a)], inc=("vS", 1))
                        RY[reg].append(("vS", e1))
                        ys_tok[j] = e1
                        exp_evac_tok[ei] = e1
                        if q == 3:
                            y_dma_last = sc.op("scalar", lambda e, ei=ei, j=j:
                                               e.dma_start(
                                out=yall_rows[ei * NJ + j], in_=y_s[:, j, :]),
                                wait=[("vS", e1)], inc=("dY", 16))
            sc.flush_block()

        NB = 4
        with (
            nc.sbuf_tensor("base", [128, NB, D], F32) as basef,
            nc.sbuf_tensor("yg1", [128, NB, D], F32) as yg1,
            nc.sbuf_tensor("yg2", [128, NB, D], F32) as yg2,
            nc.sbuf_tensor("osum", [128, NB, D], F32) as osum,
        ):
            y_all_done = sc.cnt["dY"]
            tok_add = {}
            tok_st = {}
            for tt in range(NTT):
                b = tt % NB
                wb = t_conv_done + [("vS", tok_add.get(tt - NB, 0))]
                t_b = sc.op("sync", lambda e, tt=tt, b=b: e.dma_start(
                    out=basef[:, b, :], in_=out_rows[tt]),
                    wait=wb, inc=("fb%d" % b, 16))
                wg1 = [("dY", y_all_done), ("vS", tok_add.get(tt - NB, 0))]
                t_g1 = sc.op("gpsimd", lambda e, tt=tt, b=b: e.indirect_dma_start(
                    out=yg1[:, b, :], out_offset=None,
                    in_=yall_d.ap(),
                    in_offset=IndirectOffsetOnAxis(g1i[:, tt:tt + 1], 0)),
                    wait=wg1, inc=("fg%d" % b, 16))
                t_g2 = sc.op("gpsimd", lambda e, tt=tt, b=b: e.indirect_dma_start(
                    out=yg2[:, b, :], out_offset=None,
                    in_=yall_d.ap(),
                    in_offset=IndirectOffsetOnAxis(g2i[:, tt:tt + 1], 0)),
                    inc=("fg%d" % b, 16))
                wadd = [("fb%d" % b, t_b), ("fg%d" % b, t_g2)]
                if tt >= NB:
                    wadd.append(tok_st[tt - NB])
                ta = sc.op("vector", lambda e, b=b: e.tensor_add(
                    osum[:, b, :], basef[:, b, :], yg1[:, b, :]), wait=wadd,
                    inc=("vS", 1))
                tok_add[tt] = sc.op("vector", lambda e, b=b: e.tensor_add(
                    osum[:, b, :], osum[:, b, :], yg2[:, b, :]),
                    wait=[("vS", ta)], inc=("vS", 1))
                stn = "fs%d" % b
                tok_st[tt] = (stn, sc.op("scalar", lambda e, tt=tt, b=b: e.dma_start(
                    out=out_rows[tt], in_=osum[:, b, :]),
                    wait=[("vS", tok_add[tt])], inc=(stn, 16)))
            sc.op("sync", wait=[("fs%d" % b, sc.cnt["fs%d" % b])
                                for b in range(NB)])
            sc.flush_block()

    return nc


def _layout_inputs(inputs):
    x = np.ascontiguousarray(np.asarray(inputs["L0_attn_x"], dtype=np.float32))
    w1T = np.ascontiguousarray(
        np.asarray(inputs["L0_ffn_w1"], dtype=np.float32).transpose(0, 2, 1))
    w3T = np.ascontiguousarray(
        np.asarray(inputs["L0_ffn_w3"], dtype=np.float32).transpose(0, 2, 1))
    w2T = np.ascontiguousarray(
        np.asarray(inputs["L0_ffn_w2"], dtype=np.float32).transpose(0, 2, 1))
    gwT = np.ascontiguousarray(
        np.asarray(inputs["L0_ffn_gate_w"], dtype=np.float32).T)

    def prm2(a):
        a = np.asarray(a, dtype=np.float32).reshape(D, 2)
        return np.ascontiguousarray(a.reshape(NDC, 128, 2).transpose(1, 0, 2)
                                    .reshape(128, 2 * NDC))

    delta_t = prm2(inputs["L0_attn_delta"])
    alpha_t = prm2(inputs["L0_attn_alpha"])
    beta_t = prm2(inputs["L0_attn_beta"])
    gamma_t = prm2(inputs["L0_attn_gamma"])
    omega_t = np.ascontiguousarray(
        np.asarray(inputs["L0_attn_omega"], dtype=np.float32).reshape(NDC, 128).T)
    shared = dict(w1t=w1T, w3t=w3T, w2t=w2T, gwt=gwT, delta_t=delta_t,
                  alpha_t=alpha_t, beta_t=beta_t, gamma_t=gamma_t, omega_t=omega_t)
    in_maps = [dict(shared, x=np.ascontiguousarray(x[b])) for b in range(8)]
    return in_maps


def kernel(**inputs):
    in_maps = _layout_inputs(inputs)
    nc = build_nc(milestone=3)
    res = run_bass_kernel_spmd(nc, in_maps, core_ids=list(range(8)))
    return np.stack([np.asarray(r["out"]) for r in res.results], axis=0)


if __name__ == "__main__":
    import sys
    import time
    ms = int(sys.argv[1]) if len(sys.argv) > 1 else 3
    t0 = time.time()
    nc = build_nc(milestone=ms)
    n = sum(1 for _ in nc.all_instructions())
    print(f"milestone {ms}: built in {time.time()-t0:.1f}s, {n} instructions")
